# version 03
# baseline (speedup 1.0000x reference)
import math
import os
import numpy as np
import ml_dtypes
from contextlib import ExitStack

import concourse.bass as bass
import concourse.mybir as mybir
from concourse.bass_utils import run_bass_kernel_spmd

F32 = mybir.dt.float32
BF16 = mybir.dt.bfloat16
I32 = mybir.dt.int32
AF = mybir.ActivationFunctionType
ALU = mybir.AluOpType
AXX = mybir.AxisListType.X
BF = ml_dtypes.bfloat16

D = 2048
NT = 8
TPC = 1024
KC = 16
NH = 16
HD = 128
NE = 8
IE = 1408
NCI_E = 11
ISH = 5632
NCI_S = 44
I1 = 5504
NCI_1 = 43
NCORE = 8
EPS = 1e-5
RW = 32

XN = 0
GS = 16384
HB = 17408
WB = 29696
GU = 54272
A_KV = 16384
A_Q = 32768
A_PT = 49152
A_LI = 51200
A_LB = 52224
A_ST = 53248
ARENA = 62464


class Sem:
    def __init__(self, h):
        self.h = h
        self.v = 0

    def bump(self, n):
        self.v += n
        return self.v


class B:
    ENGINES = ("sync", "scalar", "tensor", "vector", "gpsimd")

    def __init__(self, nc, ctx):
        self.nc = nc
        self.ctx = ctx
        self.ops = {e: [] for e in self.ENGINES}
        self._sems = {}
        self.t = {}
        self.x_dep = [None] * NT

    def sem(self, name):
        if name not in self._sems:
            h = self.ctx.enter_context(self.nc.semaphore(name))
            self._sems[name] = Sem(h)
        return self._sems[name]

    def emit(self, eng, fn):
        self.ops[eng].append(fn)

    def wait(self, eng, sem, val):
        if sem is not None and val > 0:
            self.emit(eng, lambda e, s=sem.h, v=val: e.wait_ge(s, v))

    def dma(self, q, out, in_, sem, slow=False):
        tgt = sem.bump(16)
        self.emit(q, lambda e, o=out, i=in_, s=sem.h, sl=slow:
                  e.dma_start(out=o, in_=i, allow_slow_non_contiguous=sl).then_inc(s, 16))
        return tgt

    def dmat(self, q, out, in_, sem):
        tgt = sem.bump(16)
        self.emit(q, lambda e, o=out, i=in_, s=sem.h:
                  e.dma_start_transpose(o, i).then_inc(s, 16))
        return tgt

    def sop(self, eng, fn, sem):
        self.wait(eng, sem, sem.v)
        return self.op(eng, fn, sem)

    def op(self, eng, fn, sem=None):
        if sem is None:
            self.emit(eng, fn)
            return None
        tgt = sem.bump(1)
        self.emit(eng, lambda e, f=fn, s=sem.h: f(e).then_inc(s, 1))
        return tgt

    def replay(self, block):
        for name in self.ENGINES:
            fns = self.ops[name]
            getattr(block, name)(lambda eng, fns=fns: [fn(eng) for fn in fns])


def emit_col_from_vec(b, src_row_ap, dst_cols, n, sload, tmp_row):
    t = b.t
    PS = t["PS"]
    spe = b.sem("prep_pe")
    sve = b.sem("prep_ve")
    b.wait("sync", spe, spe.v)
    b.dma("sync", tmp_row[0:1, :], src_row_ap, sload)
    b.wait("tensor", sload, sload.v)
    for k in range(n):
        b.op("tensor", lambda e, k=k:
             e.transpose(t["PS"][:, 3072:3073], tmp_row[0:1, k * 128:(k + 1) * 128],
                         t["identf"][0:1, 0:1]), spe)
        b.wait("vector", spe, spe.v)
        b.op("vector", lambda e, k=k:
             e.tensor_copy(out=dst_cols[:, k:k + 1], in_=t["PS"][:, 3072:3073]), sve)
        b.wait("tensor", sve, sve.v)


def emit_prep(b, parts):
    t = b.t
    PS = t["PS"]
    s = b.sem("prep")
    spe = b.sem("prep_pe")
    sve = b.sem("prep_ve")
    b.dma("sync", t["identf"][:, :], t["in_identf"], s)
    b.op("vector", lambda e: e.memset(t["epscol"][:, :], EPS), b.sem("prep_ve"))
    need_bf = {"attn0", "attn1", "mat1", "ffn1"} & set(parts)
    if need_bf:
        b.dma("sync", t["identb"][:, :], t["in_identb"], s)
        b.dma("sync", t["onesrow"][0:1, :], t["in_onesrow"], s)
        b.dma("sync", t["ones128"][:, :], t["in_ones128"], s)
        b.dma("sync", t["onesrowf"][0:1, :], t["in_onesrowf"], s)
    if {"attn0", "attn1"} & set(parts):
        b.dma("sync", t["masks"][:, :].rearrange("p (m q) -> p m q", q=128),
              t["in_masks"].rearrange("m p q -> p m q"), s)
    if "attn0" in parts:
        emit_col_from_vec(b, t["in_bqkv"][0:D].rearrange("(o d) -> o d", o=1), t["bq"], KC, s, t["prow"])
        emit_col_from_vec(b, t["in_bqkv"][D:2 * D].rearrange("(o d) -> o d", o=1), t["bk"], KC, s, t["prow"])
    if "ffn0" in parts:
        b.dma("sync", t["gwraw"], t["in_gate_w"], s)
        emit_col_from_vec(b, t["in_sgw"].rearrange("d one -> one d"), t["sgwc"], KC, s, t["prow"])
        b.wait("tensor", s, s.v)
        for kc in range(KC):
            b.op("tensor", lambda e, kc=kc:
                 e.transpose(PS[:, 3072:3080], t["gwraw"][:, kc * 128:(kc + 1) * 128],
                             t["identf"][0:8, 0:8]), spe)
            b.wait("vector", spe, spe.v)
            b.op("vector", lambda e, kc=kc:
                 e.tensor_copy(out=t["gwt"][:, kc * 9:kc * 9 + 8], in_=PS[:, 3072:3080]), sve)
            b.op("vector", lambda e, kc=kc:
                 e.tensor_copy(out=t["gwt"][:, kc * 9 + 8:kc * 9 + 9], in_=t["sgwc"][:, kc:kc + 1]), sve)
            b.wait("tensor", sve, sve.v)
    if "mat1" in parts:
        b.dma("sync", t["lutrow"][0:1, :], t["in_lut"].rearrange("(o k) -> o k", o=1), s)
        b.wait("tensor", s, s.v)
        b.op("tensor", lambda e:
             e.matmul(PS[:, 3584:3600], t["onesrowf"][0:1, :], t["lutrow"][0:1, :],
                      start=True, stop=True), spe)
        b.wait("vector", spe, spe.v)
        b.op("vector", lambda e:
             e.tensor_copy(out=t["lutb"][:, :], in_=PS[:, 3584:3600]), sve)
        b.wait("tensor", sve, sve.v)
    return s, spe, sve


def emit_ln(b, prep_waits=(), router=False):
    t = b.t
    ar = t["ar"]
    PS = t["PS"]
    sv = b.sem("ln_v")
    ss = b.sem("ln_s")
    st_ = b.sem("ln_t")
    se = b.sem("ln_e")
    sr = b.sem("ln_r")
    ev_per = 2 if router else 1
    pending = None
    def flush_mm():
        nonlocal pending
        if pending is None:
            return
        j, kc, slot, evt = pending
        pending = None
        b.wait("tensor", se, evt)
        bank_first = (kc == 0 and j % 4 == 0)
        bank_last = (kc == KC - 1 and j % 4 == 3)
        b.op("tensor", lambda e, j=j, kc=kc, slot=slot, bf=bank_first, bl=bank_last:
             e.matmul(PS[0:9, 2048 + j * 128:2048 + j * 128 + 128],
                      t["gwt"][:, kc * 9:(kc + 1) * 9],
                      t["xf"][:, slot * 128:slot * 128 + 128],
                      start=(kc == 0), stop=(kc == KC - 1)), sr)
    for j in range(NT):
        xt = t["x"][:, j * D:(j + 1) * D]
        stt = t["stat"]
        s1 = stt[:, 8 * j + 0:8 * j + 1]
        s2 = stt[:, 8 * j + 1:8 * j + 2]
        m = stt[:, 8 * j + 2:8 * j + 3]
        msq = stt[:, 8 * j + 3:8 * j + 4]
        var = stt[:, 8 * j + 4:8 * j + 5]
        rstd = stt[:, 8 * j + 5:8 * j + 6]
        mr = stt[:, 8 * j + 6:8 * j + 7]
        sq = t["sq"][:, (j % 2) * D:(j % 2) * D + D]
        dep = b.x_dep[j]
        if dep is not None:
            b.wait("vector", dep[0], dep[1])
            b.wait("scalar", dep[0], dep[1])
        if j < 2:
            for ws, wv in prep_waits:
                b.wait("vector", ws, wv)
                b.wait("scalar", ws, wv)
        b.wait("vector", st_, st_.v - 16)
        b.wait("scalar", st_, st_.v - 16)
        b.sop("vector", lambda e, o=s1, i=xt: e.reduce_sum(o, i, axis=AXX), sv)
        b.sop("scalar", lambda e, i=xt, o=sq, a=s2:
              e.activation(out=o, in_=i, func=AF.Square, accum_out=a), ss)
        b.wait("vector", ss, ss.v)
        b.sop("vector", lambda e, o=m, i=s1: e.tensor_scalar_mul(o, i, 1.0 / D), sv)
        b.sop("vector", lambda e, o=msq, i=m: e.tensor_mul(o, i, i), sv)
        b.sop("vector", lambda e, o=var, i=s2, q=msq:
              e.tensor_scalar(o, i, 1.0 / D, q, ALU.mult, ALU.subtract), sv)
        b.wait("scalar", sv, sv.v)
        b.sop("scalar", lambda e, o=rstd, i=var:
              e.activation(out=o, in_=i, func=AF.Ln, bias=t["epscol"][:, :]), ss)
        b.sop("scalar", lambda e, o=rstd, i=rstd:
              e.activation(out=o, in_=i, func=AF.Exp, scale=-0.5), ss)
        b.wait("vector", ss, ss.v)
        b.sop("vector", lambda e, o=mr, i=m, r=rstd: e.tensor_mul(o, i, r), sv)
        b.sop("vector", lambda e, o=sq, i=xt, r=rstd, z=mr:
              e.tensor_scalar(o, i, r, z, ALU.mult, ALU.subtract), sv)
        b.wait("tensor", sv, sv.v)
        for kc in range(KC):
            n = st_.v
            pcol = 3072 + (n % 2) * 512
            b.wait("tensor", se, se.v - ev_per)
            b.op("tensor", lambda e, kc=kc, p=pcol, i=sq:
                 e.transpose(PS[:, p:p + 128], i[:, kc * 128:(kc + 1) * 128],
                             t["identf"][:, :]), st_)
            flush_mm()
            b.wait("vector", st_, st_.v)
            b.op("vector", lambda e, kc=kc, j=j, p=pcol:
                 e.tensor_copy(out=ar[:, XN + kc * TPC + j * 128:XN + kc * TPC + j * 128 + 128],
                               in_=PS[:, p:p + 128]), se)
            if router:
                slot = n % 2
                b.op("vector", lambda e, p=pcol, slot=slot:
                     e.tensor_copy(out=t["xf"][:, slot * 128:slot * 128 + 128],
                                   in_=PS[:, p:p + 128]), se)
                pending = (j, kc, slot, se.v)
    flush_mm()
    return (se, se.v), ((sr, sr.v) if router else None)


def emit_router(b, ln, lnr):
    t = b.t
    PS = t["PS"]
    spe = b.sem("rt_pe")
    sv = b.sem("rt_v")
    ss = b.sem("rt_s")
    b.wait("vector", lnr[0], lnr[1])
    b.op("vector", lambda e: e.tensor_copy(out=t["lgT"], in_=PS[0:9, 2048:3072]), sv)
    b.wait("tensor", sv, sv.v)
    for j in range(NT):
        r = t["route"][:, j * RW:(j + 1) * RW]
        b.op("tensor", lambda e, j=j:
             e.transpose(PS[:, 3072:3081], t["lgT"][:, j * 128:(j + 1) * 128],
                         t["identf"][0:9, 0:9]), spe)
        b.wait("vector", spe, spe.v)
        b.op("vector", lambda e, r=r: e.tensor_copy(out=r[:, 0:9], in_=PS[:, 3072:3081]), sv)
        b.wait("tensor", sv, sv.v)
        lg = r[:, 0:9]
        probs = r[:, 9:17]
        t1 = r[:, 17:25]
        sg = r[:, 25:26]
        nm = r[:, 26:27]
        ssum = r[:, 27:28]
        rs = r[:, 28:29]
        m1 = r[:, 29:30]
        m2 = r[:, 30:31]
        rd = r[:, 31:32]
        b.sop("vector", lambda e, o=nm, i=lg: e.reduce_max(o, i[:, 0:8], axis=AXX), sv)
        b.sop("vector", lambda e, o=nm: e.tensor_scalar_mul(o, o, -1.0), sv)
        b.wait("scalar", sv, sv.v)
        b.sop("scalar", lambda e, o=probs, i=lg, z=nm, a=ssum:
              e.activation(out=o, in_=i[:, 0:8], func=AF.Exp, bias=z, accum_out=a), ss)
        b.sop("scalar", lambda e, o=sg, i=lg:
              e.activation(out=o, in_=i[:, 8:9], func=AF.Sigmoid), ss)
        b.sop("scalar", lambda e, o=sg, i=sg: e.mul(o, i, 0.5), ss)
        b.wait("vector", ss, ss.v)
        b.sop("vector", lambda e, o=rs, i=ssum: e.reciprocal(o, i), sv)
        b.sop("vector", lambda e, o=probs, i=probs, s_=rs: e.tensor_scalar_mul(o, i, s_), sv)
        b.sop("vector", lambda e, o=m1, i=probs: e.reduce_max(o, i, axis=AXX), sv)
        b.sop("vector", lambda e, o=t1, i=probs, s_=m1:
              e.tensor_scalar(o, i, s_, None, ALU.is_equal), sv)
        b.sop("vector", lambda e, o=lg, i=t1, p=probs:
              e.scalar_tensor_tensor(o[:, 0:8], i, -1e30, p, ALU.mult, ALU.add), sv)
        b.sop("vector", lambda e, o=m2, i=lg: e.reduce_max(o, i[:, 0:8], axis=AXX), sv)
        b.sop("vector", lambda e, o=probs, i=lg, s_=m2:
              e.tensor_scalar(o, i[:, 0:8], s_, None, ALU.is_equal), sv)
        b.sop("vector", lambda e, o=rd, a=m1, c=m2: e.tensor_add(o, a, c), sv)
        b.sop("vector", lambda e, o=rd, i=rd: e.reciprocal(o, i), sv)
        b.sop("vector", lambda e, o=t1, i=t1, s_=m1: e.tensor_scalar_mul(o, i, s_), sv)
        b.sop("vector", lambda e, o=t1, i=probs, s_=m2, a=t1:
              e.scalar_tensor_tensor(o, i, s_, a, ALU.mult, ALU.add), sv)
        b.sop("vector", lambda e, o=probs, i=t1, s_=rd:
              e.tensor_scalar(o, i, s_, 0.5, ALU.mult, ALU.mult), sv)
    return sv, sv.v, ss, ss.v


def emit_ffn0(b, ln, rt):
    t = b.t
    ar = t["ar"]
    PS = t["PS"]
    swl = [b.sem("f0_wl0"), b.sem("f0_wl1")]
    swd = b.sem("f0_wd")
    spe = b.sem("f0_pe")
    ssl = b.sem("f0_sl")
    svh = b.sem("f0_h")
    sxe = b.sem("f0_xe")
    rt_v, rt_vv, rt_s, rt_sv = rt
    if not hasattr(b, "down_hist"):
        b.down_hist = []
    if not hasattr(b, "gu_si"):
        b.gu_si = 0

    def gu_slice(par, w):
        c = GU + (w * 2 + par) * D
        return ar[:, c:c + D]

    def emit_gu_load(par, srcs, ci, upe_done):
        if upe_done:
            b.wait("gpsimd", spe, upe_done)
        for w in (0, 1):
            sl = srcs[w](ci)
            dst = gu_slice(par, w).rearrange("p (k q) -> p k q", q=128)
            b.dma("gpsimd", dst, sl, swl[par])

    def emit_gu_compute(h_col):
        par = b.gu_si % 2
        b.gu_si += 1
        b.wait("tensor", swl[par], swl[par].v)
        b.wait("tensor", ssl, ssl.v)
        for kc in range(KC):
            for nh2 in range(2):
                b.op("tensor", lambda e, kc=kc, nh2=nh2, par=par:
                     e.matmul(PS[:, 2048 + nh2 * 512:2560 + nh2 * 512],
                              gu_slice(par, 0)[:, kc * 128:(kc + 1) * 128],
                              ar[:, XN + kc * TPC + nh2 * 512:XN + kc * TPC + nh2 * 512 + 512],
                              start=(kc == 0), stop=(kc == KC - 1)),
                     spe if (kc == KC - 1 and nh2 == 1) else None)
        gdone = spe.v
        b.wait("tensor", svh, svh.v)
        for kc in range(KC):
            for nh2 in range(2):
                b.op("tensor", lambda e, kc=kc, nh2=nh2, par=par:
                     e.matmul(PS[:, 3072 + nh2 * 512:3584 + nh2 * 512],
                              gu_slice(par, 1)[:, kc * 128:(kc + 1) * 128],
                              ar[:, XN + kc * TPC + nh2 * 512:XN + kc * TPC + nh2 * 512 + 512],
                              start=(kc == 0), stop=(kc == KC - 1)),
                     spe if (kc == KC - 1 and nh2 == 1) else None)
        udone = spe.v
        b.wait("scalar", spe, gdone)
        b.wait("scalar", svh, svh.v)
        b.op("scalar", lambda e:
             e.activation(out=ar[:, GS:GS + TPC], in_=PS[:, 2048:3072], func=AF.Silu), ssl)
        b.wait("vector", spe, udone)
        b.wait("vector", ssl, ssl.v)
        b.op("vector", lambda e, h=h_col:
             e.tensor_tensor(out=ar[:, h:h + TPC], in0=PS[:, 3072:4096],
                             in1=ar[:, GS:GS + TPC], op=ALU.mult), svh)
        return udone

    def emit_down(nci, hbase, wbase, scal_fn, wd_ready, xwait=None):
        b.wait("tensor", svh, svh.v)
        b.wait("tensor", swd, wd_ready)
        for st2 in range(16):
            s8, dh = st2 // 2, st2 % 2
            bank = dh * 1024
            if len(b.down_hist) >= 2:
                b.wait("tensor", sxe, b.down_hist[-2])
            for ci2 in range(nci):
                for n5 in range(2):
                    b.op("tensor", lambda e, ci2=ci2, s8=s8, dh=dh, n5=n5, bank=bank,
                         hb=hbase, wb=wbase, nci=nci:
                         e.matmul(PS[:, bank + n5 * 512:bank + n5 * 512 + 512],
                                  ar[:, hb + ci2 * TPC + s8 * 128:hb + ci2 * TPC + s8 * 128 + 128],
                                  ar[:, wb + ci2 * D + dh * 1024 + n5 * 512:wb + ci2 * D + dh * 1024 + n5 * 512 + 512],
                                  start=(ci2 == 0), stop=(ci2 == nci - 1)),
                         spe if (ci2 == nci - 1 and n5 == 1) else None)
            b.wait("vector", spe, spe.v)
            if xwait is not None:
                b.wait("vector", xwait[0], xwait[1])
                xwait = None
            xh = t["x"][:, s8 * D + dh * 1024:s8 * D + dh * 1024 + 1024]
            sc = scal_fn(s8)
            if sc is None:
                b.op("vector", lambda e, p0=bank, xh=xh:
                     e.scalar_tensor_tensor(xh, PS[:, p0:p0 + 1024], 0.5, xh,
                                            ALU.mult, ALU.add), sxe)
            else:
                b.op("vector", lambda e, p0=bank, xh=xh, sc=sc:
                     e.scalar_tensor_tensor(xh, PS[:, p0:p0 + 1024], sc, xh,
                                            ALU.mult, ALU.add), sxe)
            b.down_hist.append(sxe.v)
            b.x_dep[s8] = (sxe, sxe.v)

    b.wait("tensor", rt_v, rt_vv)
    b.wait("tensor", ln[0], ln[1])

    esrcs = {}
    for ee in range(NE):
        esrcs[ee] = (
            lambda ci, ee=ee: t["in_wg"][ee, :, ci * 128:(ci + 1) * 128].rearrange("(k p) q -> p k q", p=128),
            lambda ci, ee=ee: t["in_wu"][ee, :, ci * 128:(ci + 1) * 128].rearrange("(k p) q -> p k q", p=128),
        )

    def wd_load(ee):
        for c0, c1 in ((0, 4), (4, 8), (8, 11)):
            sl = t["in_wd"][ee, c0 * 128:c1 * 128, :].rearrange("(k p) dd -> p k dd", p=128)
            dst = ar[:, WB + c0 * D:WB + c1 * D].rearrange("p (k dd) -> p k dd", dd=D)
            b.dma("gpsimd", dst, sl, swd)

    NE_RUN = 1 if os.environ.get("K_ONE_EXPERT", "") == "1" else NE
    if os.environ.get("K_NO_EXPERTS", "") == "1":
        NE_RUN = 0
    steps = [(ee, ci) for ee in range(NE_RUN) for ci in range(NCI_E)]
    upe_hist = {}
    wd_ready = 0
    b.wait("gpsimd", sxe, sxe.v)
    if NE_RUN:
        emit_gu_load(b.gu_si % 2, esrcs[0], 0, 0)
        emit_gu_load((b.gu_si + 1) % 2, esrcs[0], 1, 0)
        wd_load(0)
        wd_ready = swd.v
    si0 = b.gu_si
    for ee in range(NE_RUN):
        for ci in range(NCI_E):
            si = b.gu_si - si0
            udone = emit_gu_compute(HB + ci * TPC)
            upe_hist[si] = udone
            nxt = si + 2
            if nxt < len(steps):
                emit_gu_load((si0 + nxt) % 2, esrcs[steps[nxt][0]], steps[nxt][1],
                             upe_hist[si])
        emit_down(NCI_E, HB, WB,
                  lambda s8, ee=ee: t["route"][:, s8 * RW + 9 + ee:s8 * RW + 10 + ee],
                  wd_ready,
                  xwait=(rt_v, rt_vv) if ee == 0 else None)
        if ee + 1 < NE_RUN:
            b.wait("gpsimd", spe, spe.v)
            wd_load(ee + 1)
            wd_ready = swd.v

    if os.environ.get("K_NO_SHARED", "") == "1":
        return
    blocks = [list(range(s0, min(s0 + 6, NCI_S))) for s0 in range(0, NCI_S, 6)]
    ssrc = (
        lambda ci: t["in_wsg"][:, ci * 128:(ci + 1) * 128].rearrange("(k p) q -> p k q", p=128),
        lambda ci: t["in_wsu"][:, ci * 128:(ci + 1) * 128].rearrange("(k p) q -> p k q", p=128),
    )

    def wsd_load(bi):
        cis = blocks[bi]
        base = WB + (bi % 2) * 6 * D
        for k, ci in enumerate(cis):
            b.dma("gpsimd", ar[:, base + k * D:base + (k + 1) * D],
                  t["in_wsd"][ci * 128:(ci + 1) * 128, :], swd)

    si0 = b.gu_si
    emit_gu_load(b.gu_si % 2, ssrc, 0, upe_hist.get(len(steps) - 2, 0))
    emit_gu_load((b.gu_si + 1) % 2, ssrc, 1, upe_hist.get(len(steps) - 1, 0))
    b.wait("gpsimd", spe, spe.v)
    wsd_load(0)
    wsd_tgt = swd.v
    down_pe_done = {}
    upe_hist = {}
    ci_global = 0
    for bi, cis in enumerate(blocks):
        hb = HB + (bi % 2) * 6 * TPC
        for k, ci in enumerate(cis):
            si = b.gu_si - si0
            udone = emit_gu_compute(hb + k * TPC)
            upe_hist[si] = udone
            nxt_ci = ci_global + 2
            if nxt_ci < NCI_S:
                emit_gu_load((si0 + si + 2) % 2, ssrc, nxt_ci, upe_hist[si])
            ci_global += 1
        emit_down(len(cis), hb, WB + (bi % 2) * 6 * D,
                  lambda s8: t["route"][:, s8 * RW + 25:s8 * RW + 26],
                  wsd_tgt,
                  xwait=(rt_s, rt_sv) if bi == 0 else None)
        down_pe_done[bi] = spe.v
        if bi + 1 < len(blocks):
            if bi >= 1:
                b.wait("gpsimd", spe, down_pe_done[bi - 1])
            wsd_load(bi + 1)
            wsd_tgt = swd.v



def emit_attn(b, ln, layer):
    t = b.t
    ar = t["ar"]
    PS = t["PS"]
    x = t["x"]
    ISQ = 1.0 / math.sqrt(HD)
    spa = b.sem("at_pe")
    sse = b.sem("at_se")
    sva = b.sem("at_ve")
    sxe = b.sem("f0_xe")
    swq = [b.sem("at_wq0"), b.sem("at_wq1")]
    swf = b.sem("at_wf")
    skh = [b.sem("at_kh0"), b.sem("at_kh1")]
    svh2 = [b.sem("at_vh0"), b.sem("at_vh1")]
    scc = b.sem("at_cc")
    sct = b.sem("at_ct")

    if layer == 0:
        wq_src = lambda oc: t["in_wqkv"][:, oc * 128:(oc + 1) * 128]
        wk_src = lambda oc: t["in_wqkv"][:, D + oc * 128:D + oc * 128 + 128]
        wv_half = lambda kc, vh_: t["in_wqkv"][kc * 128:(kc + 1) * 128,
                                               2 * D + vh_ * TPC:2 * D + (vh_ + 1) * TPC]
        wo_half = lambda kc, dh: t["in_wout"][kc * 128:(kc + 1) * 128, dh * TPC:(dh + 1) * TPC]
        ccki, ccko = t["ccK0i"], t["ccK0o"]
        ccvi, ccvo = t["ccV0i"], t["ccV0o"]
    else:
        wq_src = lambda oc: t["in_wq1"][:, oc * 128:(oc + 1) * 128]
        wk_src = lambda oc: t["in_wk1"][:, oc * 128:(oc + 1) * 128]
        wv_half = lambda kc, vh_: t["in_wv1"][kc * 128:(kc + 1) * 128, vh_ * TPC:(vh_ + 1) * TPC]
        wo_half = lambda kc, dh: t["in_wo1"][kc * 128:(kc + 1) * 128, dh * TPC:(dh + 1) * TPC]
        ccki, ccko = t["ccK1i"], t["ccK1o"]
        ccvi, ccvo = t["ccV1i"], t["ccV1o"]

    brow = ar[0:1, A_PT:A_PT + D]
    ropec = t["sq"][:, 2048:3072]
    ropes = t["sq"][:, 3072:4096]

    def stream(sl):
        return ar[:, A_ST + sl * D:A_ST + (sl + 1) * D]

    def fm_evac(oc, bias_col, dest):
        pc = 1024 * (oc % 2)
        if layer == 0:
            b.op("scalar", lambda e, pc=pc, oc=oc, d=dest:
                 e.activation(out=ar[:, d + oc * TPC:d + (oc + 1) * TPC],
                              in_=PS[:, pc:pc + TPC], func=AF.Identity,
                              bias=bias_col[:, oc:oc + 1]), sse)
            return (sse, sse.v)
        sq = t["sq"]
        b.sop("vector", lambda e, pc=pc:
              e.tensor_tensor(out=sq[:, 0:TPC], in0=PS[:, pc:pc + TPC],
                              in1=ropec, op=ALU.mult), sva)
        b.sop("vector", lambda e, pc=pc:
              e.tensor_tensor(out=sq[0:64, TPC:2 * TPC], in0=PS[64:128, pc:pc + TPC],
                              in1=ropes[0:64, :], op=ALU.mult), sva)
        b.sop("vector", lambda e, pc=pc:
              e.tensor_tensor(out=sq[64:128, TPC:2 * TPC], in0=PS[0:64, pc:pc + TPC],
                              in1=ropes[64:128, :], op=ALU.mult), sva)
        psum_free = sva.v
        b.sop("vector", lambda e, oc=oc, d=dest:
              e.tensor_tensor(out=ar[:, d + oc * TPC:d + (oc + 1) * TPC],
                              in0=sq[:, 0:TPC], in1=sq[:, TPC:2 * TPC],
                              op=ALU.add), sva)
        return (sva, psum_free)

    b.wait("tensor", ln[0], ln[1])
    if layer == 1:
        b.wait("sync", ln[0], ln[1])
        b.dma("sync", ropec, t["in_ropec"], b.sem("at_rp"))
        b.dma("sync", ropes, t["in_ropes"], b.sem("at_rp"))
        srp = b.sem("at_rp")
        b.wait("vector", srp, srp.v)

    spe0 = b.sem("f0_pe")
    b.wait("gpsimd", spe0, spe0.v)
    b.dma("gpsimd", stream(0).rearrange("p (k q) -> p k q", q=128),
          wk_src(0).rearrange("(k p) q -> p k q", p=128), swq[0])
    b.dma("gpsimd", stream(1).rearrange("p (k q) -> p k q", q=128),
          wk_src(1).rearrange("(k p) q -> p k q", p=128), swq[1])
    evac_done = {}
    for oc in range(KC):
        sl = oc % 2
        b.wait("tensor", swq[sl], swq[sl].v)
        if oc >= 2:
            sm, v = evac_done[oc - 2]
            b.wait("tensor", sm, v)
        pc = 1024 * (oc % 2)
        for kc in range(KC):
            for nh2 in range(2):
                b.op("tensor", lambda e, kc=kc, nh2=nh2, sl=sl, pc=pc:
                     e.matmul(PS[:, pc + nh2 * 512:pc + nh2 * 512 + 512],
                              stream(sl)[:, kc * 128:(kc + 1) * 128],
                              ar[:, XN + kc * TPC + nh2 * 512:XN + kc * TPC + nh2 * 512 + 512],
                              start=(kc == 0), stop=(kc == KC - 1)),
                     spa if (kc == KC - 1 and nh2 == 1) else None)
        mm_done = spa.v
        if oc + 2 < KC:
            b.wait("gpsimd", spa, mm_done)
            b.dma("gpsimd", stream(sl).rearrange("p (k q) -> p k q", q=128),
                  wk_src(oc + 2).rearrange("(k p) q -> p k q", p=128), swq[sl])
        b.wait("scalar" if layer == 0 else "vector", spa, mm_done)
        evac_done[oc] = fm_evac(oc, t.get("bk"), A_KV)
    b.wait("sync", sse, sse.v)
    b.wait("sync", sva, sva.v)
    b.dma("sync", ccki.rearrange("(k p) q -> p k q", p=128),
          ar[:, A_KV:A_KV + KC * TPC].rearrange("p (k q) -> p k q", q=TPC), sct)
    k_ct = sct.v
    b.wait("gpsimd", sct, k_ct)
    b.op("gpsimd", lambda e, i=ccki, o=ccko:
         e.collective_compute("AllGather", ALU.bypass,
                              replica_groups=[[0, 1], [2, 3], [4, 5], [6, 7]],
                              ins=[i[:, :]], outs=[o[:, :]]), scc)

    PH = int(os.environ.get("K_ATTN_PH", "9"))
    if PH <= 1:
        b.wait("sync", scc, scc.v)
        return
    if layer == 0:
        b.wait("gpsimd", sct, k_ct)
        b.dma("gpsimd", brow, t["in_bqkv"][2 * D:3 * D].rearrange("(o d) -> o d", o=1), swf)
    vev = {}
    for vh_ in range(2):
        if vh_ == 1:
            b.wait("gpsimd", spa, spa.v)
        for kc in range(KC):
            b.dma("gpsimd", ar[:, A_Q + kc * TPC:A_Q + (kc + 1) * TPC],
                  wv_half(kc, vh_), swf)
        wv_ready = swf.v
        b.wait("tensor", swf, wv_ready)
        if vh_ == 0:
            b.wait("tensor", sct, k_ct)
        for s8 in range(NT):
            pc = 1024 * (s8 % 2)
            if vh_ > 0 or s8 >= 2:
                b.wait("tensor", sva, vev[(vh_ * NT + s8) - 2])
            for kc in range(KC):
                for nh2 in range(2):
                    b.op("tensor", lambda e, kc=kc, nh2=nh2, s8=s8, pc=pc:
                         e.matmul(PS[:, pc + nh2 * 512:pc + nh2 * 512 + 512],
                                  ar[:, XN + kc * TPC + s8 * 128:XN + kc * TPC + s8 * 128 + 128],
                                  ar[:, A_Q + kc * TPC + nh2 * 512:A_Q + kc * TPC + nh2 * 512 + 512],
                                  start=(kc == 0), stop=(kc == KC - 1 and layer == 1)),
                         spa if (layer == 1 and kc == KC - 1 and nh2 == 1) else None)
            if layer == 0:
                for nh2 in range(2):
                    b.op("tensor", lambda e, nh2=nh2, pc=pc, vh_=vh_:
                         e.matmul(PS[:, pc + nh2 * 512:pc + nh2 * 512 + 512],
                                  t["onesrow"][0:1, :],
                                  brow[0:1, vh_ * TPC + nh2 * 512:vh_ * TPC + nh2 * 512 + 512],
                                  start=False, stop=True),
                         spa if nh2 == 1 else None)
            b.wait("vector", spa, spa.v)
            b.op("vector", lambda e, s8=s8, pc=pc, vh_=vh_:
                 e.tensor_copy(out=ar[:, A_KV + s8 * D + vh_ * TPC:A_KV + s8 * D + vh_ * TPC + TPC],
                               in_=PS[:, pc:pc + TPC]), sva)
            vev[vh_ * NT + s8] = sva.v
    b.wait("sync", sva, sva.v)
    b.dma("sync", ccvi.rearrange("(j p) q -> p j q", p=128),
          ar[:, A_KV:A_KV + NT * D].rearrange("p (j q) -> p j q", q=D), sct)
    v_ct = sct.v
    b.wait("gpsimd", sct, v_ct)
    b.op("gpsimd", lambda e, i=ccvi, o=ccvo:
         e.collective_compute("AllGather", ALU.bypass,
                              replica_groups=[[0, 1], [2, 3], [4, 5], [6, 7]],
                              ins=[i[:, :]], outs=[o[:, :]]), scc)
    cc_all = scc.v

    if PH <= 2:
        b.wait("sync", scc, scc.v)
        return
    b.wait("gpsimd", spa, spa.v)
    b.dma("gpsimd", stream(0).rearrange("p (k q) -> p k q", q=128),
          wq_src(0).rearrange("(k p) q -> p k q", p=128), swq[0])
    b.dma("gpsimd", stream(1).rearrange("p (k q) -> p k q", q=128),
          wq_src(1).rearrange("(k p) q -> p k q", p=128), swq[1])
    for oc in range(KC):
        sl = oc % 2
        b.wait("tensor", swq[sl], swq[sl].v)
        if oc >= 2:
            sm, v = evac_done[oc - 2]
            b.wait("tensor", sm, v)
        else:
            b.wait("tensor", sva, sva.v)
        pc = 1024 * (oc % 2)
        for kc in range(KC):
            for nh2 in range(2):
                b.op("tensor", lambda e, kc=kc, nh2=nh2, sl=sl, pc=pc:
                     e.matmul(PS[:, pc + nh2 * 512:pc + nh2 * 512 + 512],
                              stream(sl)[:, kc * 128:(kc + 1) * 128],
                              ar[:, XN + kc * TPC + nh2 * 512:XN + kc * TPC + nh2 * 512 + 512],
                              start=(kc == 0), stop=(kc == KC - 1)),
                     spa if (kc == KC - 1 and nh2 == 1) else None)
        mm_done = spa.v
        if oc + 2 < KC:
            b.wait("gpsimd", spa, mm_done)
            b.dma("gpsimd", stream(sl).rearrange("p (k q) -> p k q", q=128),
                  wq_src(oc + 2).rearrange("(k p) q -> p k q", p=128), swq[sl])
        b.wait("scalar" if layer == 0 else "vector", spa, mm_done)
        evac_done[oc] = fm_evac(oc, t.get("bq"), A_Q)

    if PH <= 3:
        b.wait("sync", scc, scc.v)
        return
    def kh(slot):
        return ar[:, A_ST + slot * D:A_ST + (slot + 1) * D]

    def vh(slot):
        return ar[:, A_ST + (2 + slot) * D:A_ST + (3 + slot) * D]

    def load_head(h):
        slot = h % 2
        for par in range(2):
            src = ccko[par * D + h * 128:par * D + (h + 1) * 128, :].rearrange(
                "p (g q) -> p g q", q=128)
            dst = kh(slot).rearrange("p (g r) -> p g r", r=256)[:, :, par * 128:par * 128 + 128]
            b.dma("sync", dst, src, skh[slot])
            srcv = ccvo[par * TPC:(par + 1) * TPC, h * 128:(h + 1) * 128].rearrange(
                "(g p) q -> p g q", p=128)
            dstv = vh(slot).rearrange("p (g r) -> p g r", r=256)[:, :, par * 128:par * 128 + 128]
            b.dma("sync", dstv, srcv, svh2[slot])

    if os.environ.get("K_DBG_Q", "") == "1":
        sdq = b.sem("at_dbgq")
        b.wait("sync", sse, sse.v)
        b.wait("sync", sva, sva.v)
        b.dma("sync", t["dbgq"], ar[:, A_Q:A_Q + KC * TPC], sdq)
    b.wait("sync", scc, cc_all)
    sm, v = evac_done[KC - 2]
    b.wait("sync", sm, v)
    sm, v = evac_done[KC - 1]
    b.wait("sync", sm, v)
    b.wait("sync", spa, spa.v)
    load_head(0)
    load_head(1)
    b.wait("gpsimd", spa, spa.v)
    for kc in range(KC):
        b.dma("gpsimd", ar[:, XN + kc * TPC:XN + (kc + 1) * TPC], wo_half(kc, 0), swf)
    wo0_ready = swf.v

    exp_done = {}
    av_done = {}
    av_hist = []
    exp_n = 0
    head_pe = {}
    for h in range(NH):
        slot = h % 2
        b.wait("tensor", skh[slot], skh[slot].v)
        b.wait("tensor", svh2[slot], svh2[slot].v)
        head_tail = sva.v
        for g in range(KC):
            qoff = (g // 2) * 128
            region = 2048 + (g % 2) * 1024
            W = TPC - qoff
            if g >= 2:
                b.wait("tensor", sse, exp_done[g - 2])
            slices = []
            if qoff < 512:
                slices.append((qoff, 512))
            slices.append((max(qoff, 512), 1024))
            for (a0, a1) in slices:
                b.op("tensor", lambda e, g=g, a0=a0, a1=a1, region=region, h=h,
                     qoff=qoff, slot=slot:
                     e.matmul(PS[:, region + a0:region + a1],
                              kh(slot)[:, g * 128:(g + 1) * 128],
                              ar[:, A_Q + h * TPC + a0:A_Q + h * TPC + a1],
                              start=True, stop=(a0 != qoff)), None)
            b.op("tensor", lambda e, g=g, region=region, qoff=qoff:
                 e.matmul(PS[:, region + qoff:region + qoff + 128],
                          t["identb"][:, :],
                          t["masks"][:, (g % 2) * 128:(g % 2) * 128 + 128],
                          start=False, stop=True), spa)
            smm = spa.v
            b.wait("scalar", spa, smm)
            if exp_n >= 2:
                b.wait("scalar", spa, av_hist[exp_n - 2])
            b.op("scalar", lambda e, g=g, region=region, qoff=qoff, W=W:
                 e.activation(out=ar[:, A_PT + (g % 2) * TPC + qoff:A_PT + (g % 2) * TPC + qoff + W],
                              in_=PS[:, region + qoff:region + qoff + W],
                              func=AF.Exp, scale=ISQ), sse)
            exp_done[g] = sse.v
            t["_at_last_exp"] = sse.v
            b.wait("tensor", sse, exp_done[g])
            if g == 0:
                b.wait("tensor", sva, head_tail)
            for si_, (a0, a1) in enumerate(slices):
                if a1 <= 512:
                    st2, sp2 = (g == 0), (g == 7)
                else:
                    st2, sp2 = (g == 0), (g == KC - 1)
                last = (si_ == len(slices) - 1)
                b.op("tensor", lambda e, g=g, a0=a0, a1=a1, st2=st2, sp2=sp2, slot=slot:
                     e.matmul(PS[:, a0:a1],
                              vh(slot)[:, g * 128:(g + 1) * 128],
                              ar[:, A_PT + (g % 2) * TPC + a0:A_PT + (g % 2) * TPC + a1],
                              start=st2, stop=sp2), None)
                b.op("tensor", lambda e, g=g, a0=a0, a1=a1, st2=st2, sp2=sp2:
                     e.matmul(PS[0:1, 1024 + a0:1024 + a1],
                              t["ones128"][:, :],
                              ar[:, A_PT + (g % 2) * TPC + a0:A_PT + (g % 2) * TPC + a1],
                              start=st2, stop=sp2),
                     spa if last else None)
            av_done[g] = spa.v
            av_hist.append(spa.v)
            exp_n += 1
        if h + 1 < NH:
            b.wait("sync", spa, head_pe.get(h - 1, 0))
            load_head(h + 1)
        b.wait("vector", spa, av_done[KC - 1])
        lr = t["sq"][0:1, 0:TPC]
        lb = ar[0:1, A_LB:A_LB + TPC]
        b.sop("vector", lambda e, lr=lr: e.tensor_copy(out=lr, in_=PS[0:1, 1024:2048]), sva)
        b.sop("vector", lambda e, lr=lr: e.reciprocal(lr, lr), sva)
        b.sop("vector", lambda e, lr=lr, lb=lb: e.tensor_copy(out=lb, in_=lr), sva)
        lb_done = sva.v
        b.wait("tensor", sva, lb_done)
        for n2 in range(2):
            b.op("tensor", lambda e, n2=n2, lb=lb:
                 e.matmul(PS[:, 1024 + n2 * 512:1024 + n2 * 512 + 512],
                          t["onesrow"][0:1, :], lb[0:1, n2 * 512:(n2 + 1) * 512],
                          start=True, stop=True), spa)
        bc_done = spa.v
        head_pe[h] = spa.v
        b.wait("vector", spa, bc_done)
        b.sop("vector", lambda e:
              e.tensor_copy(out=ar[:, A_LI:A_LI + TPC], in_=PS[:, 1024:2048]), sva)
        b.sop("vector", lambda e, h=h:
              e.tensor_tensor(out=ar[:, A_KV + h * TPC:A_KV + (h + 1) * TPC],
                              in0=PS[:, 0:TPC], in1=ar[:, A_LI:A_LI + TPC],
                              op=ALU.mult), sva)

    if PH <= 4:
        return
    if layer == 0:
        b.wait("gpsimd", sse, sse.v)
        b.dma("gpsimd", brow, t["in_bout"].rearrange("(o d) -> o d", o=1), swf)
    oev = {}
    for dh in range(2 if PH >= 6 else 1):
        if dh == 1 and PH >= 7:
            b.wait("gpsimd", sse, t["_at_last_exp"])
            for kc in range(KC):
                b.dma("gpsimd", ar[:, A_Q + kc * TPC:A_Q + (kc + 1) * TPC],
                      wo_half(kc, 1), swf)
        wo_ready = swf.v
        b.wait("tensor", swf, wo_ready)
        if dh == 0:
            b.wait("tensor", sva, sva.v)
        wbase = XN if (dh == 0 or PH < 7) else A_Q
        for s8 in range(NT):
            pc = 1024 * (s8 % 2)
            st3 = dh * NT + s8
            if st3 >= 2:
                b.wait("tensor", sxe, oev[st3 - 2])
            for kc in range(KC):
                for nh2 in range(2):
                    b.op("tensor", lambda e, kc=kc, nh2=nh2, s8=s8, pc=pc, wbase=wbase:
                         e.matmul(PS[:, pc + nh2 * 512:pc + nh2 * 512 + 512],
                                  ar[:, A_KV + kc * TPC + s8 * 128:A_KV + kc * TPC + s8 * 128 + 128],
                                  ar[:, wbase + kc * TPC + nh2 * 512:wbase + kc * TPC + nh2 * 512 + 512],
                                  start=(kc == 0), stop=(kc == KC - 1 and layer == 1)),
                         spa if (layer == 1 and kc == KC - 1 and nh2 == 1) else None)
            if layer == 0:
                for nh2 in range(2):
                    b.op("tensor", lambda e, nh2=nh2, pc=pc, dh=dh:
                         e.matmul(PS[:, pc + nh2 * 512:pc + nh2 * 512 + 512],
                                  t["onesrow"][0:1, :],
                                  brow[0:1, dh * TPC + nh2 * 512:dh * TPC + nh2 * 512 + 512],
                                  start=False, stop=True),
                         spa if nh2 == 1 else None)
            b.wait("vector", spa, spa.v)
            dep = b.x_dep[s8]
            if dep is not None and dh == 0:
                b.wait("vector", dep[0], dep[1])
            b.op("vector", lambda e, s8=s8, pc=pc, dh=dh:
                 e.tensor_tensor(out=x[:, s8 * D + dh * TPC:s8 * D + dh * TPC + TPC],
                                 in0=PS[:, pc:pc + TPC],
                                 in1=x[:, s8 * D + dh * TPC:s8 * D + dh * TPC + TPC],
                                 op=ALU.add), sxe)
            oev[st3] = sxe.v
            b.x_dep[s8] = (sxe, sxe.v)



def emit_mat1(b):
    t = b.t
    ar = t["ar"]
    sld = b.sem("m1_ld")
    sst = b.sem("m1_st")
    str_ = b.sem("m1_tr")
    sve = b.sem("m1_ve")
    scc = b.sem("m1_cc")
    sxe = b.sem("f0_xe")
    prep = b.sem("prep")
    b.wait("vector", prep, prep.v)
    pve = b.sem("prep_ve")
    b.wait("vector", pve, pve.v)
    b.wait("gpsimd", sxe, sxe.v)
    b.wait("sync", sxe, sxe.v)

    CB = [0, 11008]
    TT = [22016, 33024]
    WW = 44032
    MM = [55040 + 0, 55040 + 5504]
    MK = 55040

    shards = [
        ("g", t["in_gidx"], t["gbf"], 5504, 256, t["cgi"], 2),
        ("u", t["in_uidx"], t["ubf"], 5504, 256, t["cui"], 2),
        ("d", t["in_didx"], t["dbf"], 2048, 688, t["cdi"], 6),
    ]
    for i, (nm, src, dst, R, C, _, _n) in enumerate(shards):
        sl = i % 2
        if i >= 2:
            b.wait("gpsimd", sst, 16 * i)
        ncols = (R // 128) * C
        b.dma("gpsimd", ar[:, CB[sl]:CB[sl] + ncols].rearrange("p (tt c) -> p tt c", c=C),
              src.rearrange("(tt p) c -> p tt c", p=128), sld)
        b.wait("sync", sld, sld.v)
        b.dma("sync", dst.rearrange("(tt p) c -> p tt c", p=128),
              ar[:, CB[sl]:CB[sl] + ncols].rearrange("p (tt c) -> p tt c", c=C), sst)

    lutb = t["lutb"]
    tslot = 0
    tr_hist = []
    sel_hist = []

    def do_tile(bfsrc, r0, r1, c0, c1, width, out_dram_rows, cdst, wslot, trow0=0):
        nonlocal tslot
        sl = tslot % 2
        tslot += 1
        if len(tr_hist) >= 2:
            b.wait("sync", sve, sel_hist[-2])
        b.dmat("sync", ar[:, TT[sl]:TT[sl] + width], bfsrc[r0:r1, c0:c1], str_)
        tr_hist.append(str_.v)
        b.wait("vector", str_, str_.v)
        for k in range(16):
            mb = MK if width > 2048 else MM[k % 2] if False else MK
            b.sop("vector", lambda e, k=k, sl=sl, width=width, mb=mb:
                  e.tensor_scalar(ar[:, mb:mb + width], ar[:, TT[sl]:TT[sl] + width],
                                  float(k), None, ALU.is_equal), sve)
            if k == 0:
                b.sop("vector", lambda e, k=k, width=width, mb=mb:
                      e.tensor_scalar(ar[:, WW:WW + width], ar[:, mb:mb + width],
                                      lutb[:, k:k + 1], None, ALU.mult), sve)
            else:
                b.sop("vector", lambda e, k=k, width=width, mb=mb:
                      e.scalar_tensor_tensor(ar[:, WW:WW + width], ar[:, mb:mb + width],
                                             lutb[:, k:k + 1], ar[:, WW:WW + width],
                                             ALU.mult, ALU.add), sve)
        sel_hist.append(sve.v)
        b.wait("sync", sve, sve.v)
        b.dma("sync", cdst[out_dram_rows[0]:out_dram_rows[1], :],
              ar[trow0:128, WW:WW + width], sst)

    b.wait("sync", sst, 16 * 3)
    gathers = {"g": (t["cgi"], t["w1gT"]), "u": (t["cui"], t["w1uT"]),
               "d": (t["cdi"], t["w1dT"])}
    for nm, src, bf, R, C, cdst, ntile in shards:
        if nm in ("g", "u"):
            for dt in range(2):
                do_tile(bf, 0, R, dt * 128, (dt + 1) * 128, R, (dt * 128, (dt + 1) * 128), cdst, None)
        else:
            for q in range(6):
                c0 = q * 128 if q < 5 else C - 128
                rows = (q * 128, min((q + 1) * 128, C))
                do_tile(bf, 0, R, c0, c0 + 128, R, rows, cdst, None,
                        trow0=0 if q < 5 else 128 - (rows[1] - rows[0]))
        ins_, outs_ = gathers[nm]
        b.wait("gpsimd", sst, sst.v)
        b.op("gpsimd", lambda e, i_=ins_, o_=outs_:
             e.collective_compute("AllGather", ALU.bypass,
                                  replica_groups=[[0, 1, 2, 3, 4, 5, 6, 7]],
                                  ins=[i_[:, :]], outs=[o_[:, :]]), scc)


    return scc, scc.v


def emit_ffn1(b, ln, mat):
    t = b.t
    ar = t["ar"]
    PS = t["PS"]
    x = t["x"]
    swl = [b.sem("f1_wl0"), b.sem("f1_wl1")]
    swd = b.sem("f1_wd")
    spe = b.sem("f0_pe")
    ssl = b.sem("f0_sl")
    svh = b.sem("f0_h")
    sxe = b.sem("f0_xe")
    if not hasattr(b, "down_hist"):
        b.down_hist = []
    if not hasattr(b, "gu_si"):
        b.gu_si = 0

    def gu_slice(par, w):
        c = GU + (w * 2 + par) * D
        return ar[:, c:c + D]

    def gu_load(par, ci, upe_done):
        if upe_done:
            b.wait("sync", spe, upe_done)
        for w, src in ((0, t["w1gT"]), (1, t["w1uT"])):
            sl = src[:, ci * 128:(ci + 1) * 128].rearrange("(k p) q -> p k q", p=128)
            b.dma("sync", gu_slice(par, w).rearrange("p (k q) -> p k q", q=128), sl, swl[par])

    def gu_compute(h_col):
        par = b.gu_si % 2
        b.gu_si += 1
        b.wait("tensor", swl[par], swl[par].v)
        b.wait("tensor", ssl, ssl.v)
        for kc in range(KC):
            for nh2 in range(2):
                b.op("tensor", lambda e, kc=kc, nh2=nh2, par=par:
                     e.matmul(PS[:, 2048 + nh2 * 512:2560 + nh2 * 512],
                              gu_slice(par, 0)[:, kc * 128:(kc + 1) * 128],
                              ar[:, XN + kc * TPC + nh2 * 512:XN + kc * TPC + nh2 * 512 + 512],
                              start=(kc == 0), stop=(kc == KC - 1)),
                     spe if (kc == KC - 1 and nh2 == 1) else None)
        gdone = spe.v
        b.wait("tensor", svh, svh.v)
        for kc in range(KC):
            for nh2 in range(2):
                b.op("tensor", lambda e, kc=kc, nh2=nh2, par=par:
                     e.matmul(PS[:, 3072 + nh2 * 512:3584 + nh2 * 512],
                              gu_slice(par, 1)[:, kc * 128:(kc + 1) * 128],
                              ar[:, XN + kc * TPC + nh2 * 512:XN + kc * TPC + nh2 * 512 + 512],
                              start=(kc == 0), stop=(kc == KC - 1)),
                     spe if (kc == KC - 1 and nh2 == 1) else None)
        udone = spe.v
        b.wait("scalar", spe, gdone)
        b.wait("scalar", svh, svh.v)
        b.op("scalar", lambda e:
             e.activation(out=ar[:, GS:GS + TPC], in_=PS[:, 2048:3072], func=AF.Silu), ssl)
        b.wait("vector", spe, udone)
        b.wait("vector", ssl, ssl.v)
        b.op("vector", lambda e, h=h_col:
             e.tensor_tensor(out=ar[:, h:h + TPC], in0=PS[:, 3072:4096],
                             in1=ar[:, GS:GS + TPC], op=ALU.mult), svh)
        return udone

    def down(nci, hbase, wbase):
        b.wait("tensor", svh, svh.v)
        b.wait("tensor", swd, swd.v)
        for st2 in range(16):
            s8, dh = st2 // 2, st2 % 2
            bank = dh * 1024
            if len(b.down_hist) >= 2:
                b.wait("tensor", sxe, b.down_hist[-2])
            for ci2 in range(nci):
                for n5 in range(2):
                    b.op("tensor", lambda e, ci2=ci2, s8=s8, dh=dh, n5=n5, bank=bank,
                         hb=hbase, wb=wbase, nci=nci:
                         e.matmul(PS[:, bank + n5 * 512:bank + n5 * 512 + 512],
                                  ar[:, hb + ci2 * TPC + s8 * 128:hb + ci2 * TPC + s8 * 128 + 128],
                                  ar[:, wb + ci2 * D + dh * 1024 + n5 * 512:wb + ci2 * D + dh * 1024 + n5 * 512 + 512],
                                  start=(ci2 == 0), stop=(ci2 == nci - 1)),
                         spe if (ci2 == nci - 1 and n5 == 1) else None)
            b.wait("vector", spe, spe.v)
            xh = x[:, s8 * D + dh * 1024:s8 * D + dh * 1024 + 1024]
            b.op("vector", lambda e, p0=bank, xh=xh:
                 e.scalar_tensor_tensor(xh, PS[:, p0:p0 + 1024], 0.5, xh,
                                        ALU.mult, ALU.add), sxe)
            b.down_hist.append(sxe.v)
            b.x_dep[s8] = (sxe, sxe.v)

    blocks = [list(range(s0, min(s0 + 6, NCI_1))) for s0 in range(0, NCI_1, 6)]

    def wd_load(bi):
        base = WB + (bi % 2) * 6 * D
        for kk, ci in enumerate(blocks[bi]):
            b.dma("sync", ar[:, base + kk * D:base + (kk + 1) * D],
                  t["w1dT"][ci * 128:(ci + 1) * 128, :], swd)

    b.wait("tensor", ln[0], ln[1])
    b.wait("sync", mat[0], mat[1])
    b.wait("sync", sxe, sxe.v)
    b.wait("sync", svh, svh.v)
    si0 = b.gu_si
    gu_load(b.gu_si % 2, 0, 0)
    gu_load((b.gu_si + 1) % 2, 1, 0)
    wd_load(0)
    wd_tgt = swd.v
    upe_hist = {}
    down_pe = {}
    ci_global = 0
    for bi, cis in enumerate(blocks):
        hb = HB + (bi % 2) * 6 * TPC
        for kk, ci in enumerate(cis):
            si = b.gu_si - si0
            udone = gu_compute(hb + kk * TPC)
            upe_hist[si] = udone
            nxt = ci_global + 2
            if nxt < NCI_1:
                gu_load((si0 + si + 2) % 2, nxt, upe_hist[si])
            ci_global += 1
        down(len(cis), hb, WB + (bi % 2) * 6 * D)
        down_pe[bi] = spe.v
        if bi + 1 < len(blocks):
            if bi >= 1:
                b.wait("sync", spe, down_pe[bi - 1])
            wd_load(bi + 1)



def declare_common(nc, b, parts):
    t = b.t
    ctx = b.ctx
    e = ctx.enter_context
    t["xin"] = nc.dram_tensor("xin", [NT, 128, D], F32, kind="ExternalInput").ap()
    t["xout"] = nc.dram_tensor("xout", [NT, 128, D], F32, kind="ExternalOutput").ap()
    if os.environ.get("K_DBG_Q", "") == "1":
        t["dbgq"] = nc.dram_tensor("dbgq", [128, KC * TPC], BF16, kind="ExternalOutput").ap()
    if os.environ.get("K_DBG_X", "") == "1":
        t["dbgx"] = nc.dram_tensor("dbgx", [6, NT, 128, D], F32, kind="ExternalOutput").ap()
    if os.environ.get("K_DEBUG", "") == "1":
        t["dbg_stat"] = nc.dram_tensor("dbg_stat", [128, 64], F32, kind="ExternalOutput").ap()
        t["dbg_route"] = nc.dram_tensor("dbg_route", [128, NT * RW], F32, kind="ExternalOutput").ap()
        t["dbg_xn"] = nc.dram_tensor("dbg_xn", [128, KC * TPC], BF16, kind="ExternalOutput").ap()
    t["in_identf"] = nc.dram_tensor("identf", [128, 128], F32, kind="ExternalInput").ap()
    t["x"] = e(nc.sbuf_tensor("x", [128, NT * D], F32))
    t["ar"] = e(nc.sbuf_tensor("ar", [128, ARENA], BF16))
    t["PS"] = e(nc.psum_tensor("PS", [128, 4096], F32))
    t["sq"] = e(nc.sbuf_tensor("sq", [128, 2 * D], F32))
    t["stat"] = e(nc.sbuf_tensor("stat", [128, 64], F32))
    t["identf"] = e(nc.sbuf_tensor("identfs", [128, 128], F32))
    t["epscol"] = e(nc.sbuf_tensor("epscol", [128, 1], F32))
    t["prow"] = t["sq"][0:1, 2048:4096]
    if {"attn0", "attn1", "mat1", "ffn1"} & set(parts):
        t["in_identb"] = nc.dram_tensor("identb", [128, 128], BF16, kind="ExternalInput").ap()
        t["in_onesrow"] = nc.dram_tensor("onesrow", [1, 128], BF16, kind="ExternalInput").ap()
        t["in_ones128"] = nc.dram_tensor("ones128", [128, 1], BF16, kind="ExternalInput").ap()
        t["in_onesrowf"] = nc.dram_tensor("onesrowf", [1, 128], F32, kind="ExternalInput").ap()
        t["identb"] = e(nc.sbuf_tensor("identbs", [128, 128], BF16))
        t["onesrow"] = e(nc.sbuf_tensor("onesrows", [1, 128], BF16))
        t["ones128"] = e(nc.sbuf_tensor("ones128s", [128, 1], BF16))
        t["onesrowf"] = e(nc.sbuf_tensor("onesrowfs", [1, 128], F32))
    if {"attn0", "attn1"} & set(parts):
        t["in_masks"] = nc.dram_tensor("masks", [2, 128, 128], BF16, kind="ExternalInput").ap()
        t["masks"] = e(nc.sbuf_tensor("maskss", [128, 256], BF16))
    if "attn1" in parts:
        t["in_ropec"] = nc.dram_tensor("ropec", [128, TPC], F32, kind="ExternalInput").ap()
        t["in_ropes"] = nc.dram_tensor("ropes", [128, TPC], F32, kind="ExternalInput").ap()
    if "ffn0" in parts:
        t["in_gate_w"] = nc.dram_tensor("gate_w", [8, D], F32, kind="ExternalInput").ap()
        t["in_sgw"] = nc.dram_tensor("sgw", [D, 1], F32, kind="ExternalInput").ap()
        t["in_wg"] = nc.dram_tensor("wg", [NE, D, IE], F32, kind="ExternalInput").ap()
        t["in_wu"] = nc.dram_tensor("wu", [NE, D, IE], F32, kind="ExternalInput").ap()
        t["in_wd"] = nc.dram_tensor("wd", [NE, IE, D], F32, kind="ExternalInput").ap()
        t["in_wsg"] = nc.dram_tensor("wsg", [D, ISH], F32, kind="ExternalInput").ap()
        t["in_wsu"] = nc.dram_tensor("wsu", [D, ISH], F32, kind="ExternalInput").ap()
        t["in_wsd"] = nc.dram_tensor("wsd", [ISH, D], F32, kind="ExternalInput").ap()
        t["gwraw"] = t["sq"][0:8, 0:D]
        t["sgwc"] = e(nc.sbuf_tensor("sgwc", [128, KC], F32))
        t["gwt"] = e(nc.sbuf_tensor("gwt", [128, 9 * KC], F32))
        t["xf"] = e(nc.sbuf_tensor("xf", [128, 256], F32))
        t["lgT"] = t["sq"][0:9, 0:TPC]
        t["route"] = e(nc.sbuf_tensor("route", [128, NT * RW], F32))
    for L in (0, 1):
        if f"attn{L}" in parts:
            t[f"ccK{L}i"] = nc.dram_tensor(f"ccK{L}i", [D, TPC], BF16, kind="Internal").ap()
            t[f"ccK{L}o"] = nc.dram_tensor(f"ccK{L}o", [2 * D, TPC], BF16, kind="Internal").ap()
            t[f"ccV{L}i"] = nc.dram_tensor(f"ccV{L}i", [TPC, D], BF16, kind="Internal").ap()
            t[f"ccV{L}o"] = nc.dram_tensor(f"ccV{L}o", [2 * TPC, D], BF16, kind="Internal").ap()
    if "attn0" in parts:
        t["in_bqkv"] = nc.dram_tensor("bqkv", [3 * D], F32, kind="ExternalInput").ap()
        t["in_bout"] = nc.dram_tensor("bout", [D], F32, kind="ExternalInput").ap()
        t["in_wqkv"] = nc.dram_tensor("wqkv", [D, 3 * D], F32, kind="ExternalInput").ap()
        t["in_wout"] = nc.dram_tensor("wout", [D, D], F32, kind="ExternalInput").ap()
        t["bq"] = e(nc.sbuf_tensor("bq", [128, KC], F32))
        t["bk"] = e(nc.sbuf_tensor("bk", [128, KC], F32))
    if "attn1" in parts:
        t["in_wq1"] = nc.dram_tensor("wq1", [D, D], F32, kind="ExternalInput").ap()
        t["in_wk1"] = nc.dram_tensor("wk1", [D, D], F32, kind="ExternalInput").ap()
        t["in_wv1"] = nc.dram_tensor("wv1", [D, D], F32, kind="ExternalInput").ap()
        t["in_wo1"] = nc.dram_tensor("wo1", [D, D], F32, kind="ExternalInput").ap()
    if "mat1" in parts:
        t["in_lut"] = nc.dram_tensor("lut", [16], F32, kind="ExternalInput").ap()
        t["in_gidx"] = nc.dram_tensor("gidx", [I1, 256], I32, kind="ExternalInput").ap()
        t["in_uidx"] = nc.dram_tensor("uidx", [I1, 256], I32, kind="ExternalInput").ap()
        t["in_didx"] = nc.dram_tensor("didx", [D, 688], I32, kind="ExternalInput").ap()
        t["lutrow"] = e(nc.sbuf_tensor("lutrow", [1, 16], F32))
        t["lutb"] = e(nc.sbuf_tensor("lutb", [128, 16], F32))
        t["gbf"] = nc.dram_tensor("gbf", [I1, 256], BF16, kind="Internal").ap()
        t["ubf"] = nc.dram_tensor("ubf", [I1, 256], BF16, kind="Internal").ap()
        t["dbf"] = nc.dram_tensor("dbf", [D, 688], BF16, kind="Internal").ap()
        t["cgi"] = nc.dram_tensor("cgi", [256, I1], BF16, kind="Internal").ap()
        t["cui"] = nc.dram_tensor("cui", [256, I1], BF16, kind="Internal").ap()
        t["cdi"] = nc.dram_tensor("cdi", [688, D], BF16, kind="Internal").ap()
    if {"mat1", "ffn1"} & set(parts):
        t["w1gT"] = nc.dram_tensor("w1gT", [D, I1], BF16, kind="Internal").ap()
        t["w1uT"] = nc.dram_tensor("w1uT", [D, I1], BF16, kind="Internal").ap()
        t["w1dT"] = nc.dram_tensor("w1dT", [I1, D], BF16, kind="Internal").ap()


def emit_load_x(b):
    s = b.sem("x_in")
    for j in range(NT):
        b.dma("sync", b.t["x"][:, j * D:(j + 1) * D], b.t["xin"][j], s)
    for j in range(NT):
        b.x_dep[j] = (s, s.v)


def emit_store_x(b):
    s = b.sem("x_out")
    for j in range(NT):
        dep = b.x_dep[j]
        if dep is not None:
            b.wait("sync", dep[0], dep[1])
        b.dma("sync", b.t["xout"][j], b.t["x"][:, j * D:(j + 1) * D], s)
    if os.environ.get("K_DEBUG", "") == "1":
        b.dma("sync", b.t["dbg_stat"], b.t["stat"][:, :], s)
        b.dma("sync", b.t["dbg_route"], b.t["route"][:, :], s)
        b.dma("sync", b.t["dbg_xn"], b.t["ar"][:, XN:XN + KC * TPC], s)
    b.wait("sync", s, s.v)


def build_program(parts):
    nc = bass.Bass("TRN2", target_bir_lowering=False)
    ctx = ExitStack()
    with ctx:
        b = B(nc, ctx)
        declare_common(nc, b, parts)
        ps_, ppe_, pve_ = emit_prep(b, parts)
        emit_load_x(b)
        prep_w = [(ps_, ps_.v), (ppe_, ppe_.v), (pve_, pve_.v)]
        first = True
        dbgx_i = [0]

        def dump_x():
            if os.environ.get("K_DBG_X", "") != "1":
                return
            sdx = b.sem("dbgx")
            for j in range(NT):
                dep = b.x_dep[j]
                if dep is not None:
                    b.wait("sync", dep[0], dep[1])
                b.dma("sync", b.t["dbgx"][dbgx_i[0], j],
                      b.t["x"][:, j * D:(j + 1) * D], sdx)
            dbgx_i[0] += 1
        for p in parts:
            if p == "ffn0":
                ln, lnr = emit_ln(b, prep_w if first else (), router=True)
                rt = emit_router(b, ln, lnr)
                emit_ffn0(b, ln, rt)
            elif p in ("attn0", "attn1"):
                ln, _ = emit_ln(b, prep_w if first else (), router=False)
                emit_attn(b, ln, 0 if p == "attn0" else 1)
            elif p == "mat1":
                b.t["_mat"] = emit_mat1(b)
            elif p == "ffn1":
                ln, _ = emit_ln(b, prep_w if first else (), router=False)
                emit_ffn1(b, ln, b.t["_mat"])
            else:
                raise NotImplementedError(p)
            if p != "mat1":
                dump_x()
            first = False
        emit_store_x(b)
        block = ctx.enter_context(nc.Block())
        b.replay(block)
    return nc



def make_constants(core, parts):
    d = {}
    par = core % 2
    d["identf"] = np.eye(128, dtype=np.float32)
    if {"attn0", "attn1", "mat1", "ffn1"} & set(parts):
        d["identb"] = np.eye(128, dtype=BF)
        d["onesrow"] = np.ones((1, 128), dtype=BF)
        d["ones128"] = np.ones((128, 1), dtype=BF)
        d["onesrowf"] = np.ones((1, 128), dtype=np.float32)
    if {"attn0", "attn1"} & set(parts):
        tri = np.where(np.arange(128)[None, :] < np.arange(128)[:, None],
                       np.float32(-1e9), np.float32(0.0))
        zero = np.zeros((128, 128), np.float32)
        full = np.full((128, 128), -1e9, np.float32)
        meven = tri if par == 0 else zero
        modd = full if par == 0 else tri
        d["masks"] = np.stack([meven, modd]).astype(BF)
    if "attn1" in parts:
        inv = 1.0 / (10000.0 ** (np.arange(0, HD, 2, dtype=np.float32) / HD))
        tiles = 2 * np.arange(NT) + par
        pos = (tiles[:, None] * 128 + np.arange(128)[None, :]).reshape(-1).astype(np.float32)
        fr = np.outer(inv, pos)
        d["ropec"] = np.concatenate([np.cos(fr), np.cos(fr)], 0).astype(np.float32)
        sn = np.sin(fr)
        d["ropes"] = np.concatenate([-sn, sn], 0).astype(np.float32)
    return d


def shard_x(x, core):
    seq, par = core // 2, core % 2
    tiles = []
    for j in range(NT):
        g = 2 * j + par
        tiles.append(np.asarray(x[seq, g * 128:(g + 1) * 128, :], dtype=np.float32))
    return np.stack(tiles)


def unshard_x(shards):
    out = np.empty((4, 2048, 2048), dtype=np.float32)
    for c in range(NCORE):
        seq, par = c // 2, c % 2
        for j in range(NT):
            g = 2 * j + par
            out[seq, g * 128:(g + 1) * 128, :] = shards[c][j]
    return out


def ffn0_inputs(inputs):
    return {
        "gate_w": np.ascontiguousarray(inputs["L0_ffn_gate_w"], np.float32),
        "sgw": np.ascontiguousarray(inputs["L0_ffn_shared_gate_w"], np.float32),
        "wg": np.ascontiguousarray(inputs["L0_ffn_w_gate"], np.float32),
        "wu": np.ascontiguousarray(inputs["L0_ffn_w_up"], np.float32),
        "wd": np.ascontiguousarray(inputs["L0_ffn_w_down"], np.float32),
        "wsg": np.ascontiguousarray(inputs["L0_ffn_sw_gate"], np.float32),
        "wsu": np.ascontiguousarray(inputs["L0_ffn_sw_up"], np.float32),
        "wsd": np.ascontiguousarray(inputs["L0_ffn_sw_down"], np.float32),
    }


def run_launch(parts, x, inputs):
    nc = build_program(parts)
    in_maps = []
    for c in range(NCORE):
        m = {"xin": shard_x(x, c)}
        m.update(make_constants(c, parts))
        if "ffn0" in parts:
            m.update(ffn0_inputs(inputs))
        if "attn0" in parts:
            m["bqkv"] = np.ascontiguousarray(inputs["L0_attn_b_qkv"], np.float32)
            m["bout"] = np.ascontiguousarray(inputs["L0_attn_b_out"], np.float32)
            m["wqkv"] = np.ascontiguousarray(inputs["L0_attn_w_qkv"], np.float32)
            m["wout"] = np.ascontiguousarray(inputs["L0_attn_w_out"], np.float32)
        if "mat1" in parts:
            m["lut"] = np.ascontiguousarray(inputs["L1_ffn_lut"], np.float32)
            m["gidx"] = np.ascontiguousarray(inputs["L1_ffn_gate_idx"][:, c * 256:(c + 1) * 256], np.int32)
            m["uidx"] = np.ascontiguousarray(inputs["L1_ffn_up_idx"][:, c * 256:(c + 1) * 256], np.int32)
            m["didx"] = np.ascontiguousarray(inputs["L1_ffn_down_idx"][:, c * 688:(c + 1) * 688], np.int32)
        if "attn1" in parts:
            m["wq1"] = np.ascontiguousarray(inputs["L1_attn_Wq"], np.float32)
            m["wk1"] = np.ascontiguousarray(inputs["L1_attn_Wk"], np.float32)
            m["wv1"] = np.ascontiguousarray(inputs["L1_attn_Wv"], np.float32)
            m["wo1"] = np.ascontiguousarray(inputs["L1_attn_Wo"], np.float32)
        in_maps.append(m)
    res = run_bass_kernel_spmd(nc, in_maps, core_ids=list(range(NCORE)))
    return unshard_x([res.results[c]["xout"] for c in range(NCORE)])


LAUNCH_PLAN = [["ffn0", "attn0", "ffn0", "mat1", "ffn1", "attn1", "ffn1"]]


def kernel(**inputs):
    x = np.asarray(inputs["L0_attn_x"], np.float32)
    for parts in LAUNCH_PLAN:
        x = run_launch(parts, x, inputs)
    return x


# version 05
# speedup vs baseline: 1.1077x; 1.1077x over previous
import math
import os
import numpy as np
import ml_dtypes
from contextlib import ExitStack

import concourse.bass as bass
import concourse.mybir as mybir
from concourse.bass_utils import run_bass_kernel_spmd

F32 = mybir.dt.float32
BF16 = mybir.dt.bfloat16
I32 = mybir.dt.int32
AF = mybir.ActivationFunctionType
ALU = mybir.AluOpType
AXX = mybir.AxisListType.X
BF = ml_dtypes.bfloat16

D = 2048
NT = 8
TPC = 1024
KC = 16
NH = 16
HD = 128
NE = 8
IE = 1408
NCI_E = 11
ISH = 5632
NCI_S = 44
I1 = 5504
NCI_1 = 43
NCORE = 8
EPS = 1e-5
RW = 32

XN = 0
GS = 16384
HB = 17408
WB = 29696
GU = 54272
A_KV = 16384
A_Q = 32768
A_PT = 49152
A_LI = 51200
A_LB = 52224
A_ST = 53248
ARENA = 62464


class Sem:
    def __init__(self, h):
        self.h = h
        self.v = 0

    def bump(self, n):
        self.v += n
        return self.v


class B:
    ENGINES = ("sync", "scalar", "tensor", "vector", "gpsimd")

    def __init__(self, nc, ctx):
        self.nc = nc
        self.ctx = ctx
        self.ops = {e: [] for e in self.ENGINES}
        self._sems = {}
        self.t = {}
        self.x_dep = [None] * NT

    def sem(self, name):
        if name not in self._sems:
            h = self.ctx.enter_context(self.nc.semaphore(name))
            self._sems[name] = Sem(h)
        return self._sems[name]

    def emit(self, eng, fn):
        self.ops[eng].append(fn)

    def wait(self, eng, sem, val):
        if sem is not None and val > 0:
            self.emit(eng, lambda e, s=sem.h, v=val: e.wait_ge(s, v))

    def dma(self, q, out, in_, sem, slow=False):
        tgt = sem.bump(16)
        self.emit(q, lambda e, o=out, i=in_, s=sem.h, sl=slow:
                  e.dma_start(out=o, in_=i, allow_slow_non_contiguous=sl).then_inc(s, 16))
        return tgt

    def dmat(self, q, out, in_, sem):
        tgt = sem.bump(16)
        self.emit(q, lambda e, o=out, i=in_, s=sem.h:
                  e.dma_start_transpose(o, i).then_inc(s, 16))
        return tgt

    def sop(self, eng, fn, sem):
        self.wait(eng, sem, sem.v)
        return self.op(eng, fn, sem)

    def op(self, eng, fn, sem=None):
        if sem is None:
            self.emit(eng, fn)
            return None
        tgt = sem.bump(1)
        self.emit(eng, lambda e, f=fn, s=sem.h: f(e).then_inc(s, 1))
        return tgt

    def replay(self, block):
        for name in self.ENGINES:
            fns = self.ops[name]
            getattr(block, name)(lambda eng, fns=fns: [fn(eng) for fn in fns])


def emit_col_from_vec(b, src_row_ap, dst_cols, n, sload, tmp_row):
    t = b.t
    PS = t["PS"]
    spe = b.sem("prep_pe")
    sve = b.sem("prep_ve")
    b.wait("sync", spe, spe.v)
    b.dma("sync", tmp_row[0:1, :], src_row_ap, sload)
    b.wait("tensor", sload, sload.v)
    for k in range(n):
        b.op("tensor", lambda e, k=k:
             e.transpose(t["PS"][:, 3072:3073], tmp_row[0:1, k * 128:(k + 1) * 128],
                         t["identf"][0:1, 0:1]), spe)
        b.wait("vector", spe, spe.v)
        b.op("vector", lambda e, k=k:
             e.tensor_copy(out=dst_cols[:, k:k + 1], in_=t["PS"][:, 3072:3073]), sve)
        b.wait("tensor", sve, sve.v)


def emit_prep(b, parts):
    t = b.t
    PS = t["PS"]
    s = b.sem("prep")
    spe = b.sem("prep_pe")
    sve = b.sem("prep_ve")
    b.dma("sync", t["identf"][:, :], t["in_identf"], s)
    b.op("vector", lambda e: e.memset(t["epscol"][:, :], EPS), b.sem("prep_ve"))
    need_bf = {"attn0", "attn1", "mat1", "ffn1"} & set(parts)
    if need_bf:
        b.dma("sync", t["identb"][:, :], t["in_identb"], s)
        b.dma("sync", t["onesrow"][0:1, :], t["in_onesrow"], s)
        b.dma("sync", t["ones128"][:, :], t["in_ones128"], s)
        b.dma("sync", t["onesrowf"][0:1, :], t["in_onesrowf"], s)
    if {"attn0", "attn1"} & set(parts):
        b.dma("sync", t["masks"][:, :].rearrange("p (m q) -> p m q", q=128),
              t["in_masks"].rearrange("m p q -> p m q"), s)
    if "attn0" in parts:
        emit_col_from_vec(b, t["in_bqkv"][0:D].rearrange("(o d) -> o d", o=1), t["bq"], KC, s, t["prow"])
        emit_col_from_vec(b, t["in_bqkv"][D:2 * D].rearrange("(o d) -> o d", o=1), t["bk"], KC, s, t["prow"])
    if "ffn0" in parts:
        b.dma("sync", t["gwraw"], t["in_gate_w"], s)
        emit_col_from_vec(b, t["in_sgw"].rearrange("d one -> one d"), t["sgwc"], KC, s, t["prow"])
        b.wait("tensor", s, s.v)
        for kc in range(KC):
            b.op("tensor", lambda e, kc=kc:
                 e.transpose(PS[:, 3072:3080], t["gwraw"][:, kc * 128:(kc + 1) * 128],
                             t["identf"][0:8, 0:8]), spe)
            b.wait("vector", spe, spe.v)
            b.op("vector", lambda e, kc=kc:
                 e.tensor_copy(out=t["gwt"][:, kc * 9:kc * 9 + 8], in_=PS[:, 3072:3080]), sve)
            b.op("vector", lambda e, kc=kc:
                 e.tensor_copy(out=t["gwt"][:, kc * 9 + 8:kc * 9 + 9], in_=t["sgwc"][:, kc:kc + 1]), sve)
            b.wait("tensor", sve, sve.v)
    if "mat1" in parts:
        b.dma("sync", t["lutrow"][0:1, :], t["in_lut"].rearrange("(o k) -> o k", o=1), s)
        b.wait("tensor", s, s.v)
        b.op("tensor", lambda e:
             e.matmul(PS[:, 3584:3600], t["onesrowf"][0:1, :], t["lutrow"][0:1, :],
                      start=True, stop=True), spe)
        b.wait("vector", spe, spe.v)
        b.op("vector", lambda e:
             e.tensor_copy(out=t["lutb"][:, :], in_=PS[:, 3584:3600]), sve)
        b.wait("tensor", sve, sve.v)
    return s, spe, sve


def emit_ln(b, prep_waits=(), router=False):
    t = b.t
    ar = t["ar"]
    PS = t["PS"]
    sv = b.sem("ln_v")
    ss = b.sem("ln_s")
    st_ = b.sem("ln_t")
    se = b.sem("ln_e")
    sr = b.sem("ln_r")
    ev_per = 2 if router else 1
    pending = None
    def flush_mm():
        nonlocal pending
        if pending is None:
            return
        j, kc, slot, evt = pending
        pending = None
        b.wait("tensor", se, evt)
        bank_first = (kc == 0 and j % 4 == 0)
        bank_last = (kc == KC - 1 and j % 4 == 3)
        b.op("tensor", lambda e, j=j, kc=kc, slot=slot, bf=bank_first, bl=bank_last:
             e.matmul(PS[0:9, 2048 + j * 128:2048 + j * 128 + 128],
                      t["gwt"][:, kc * 9:(kc + 1) * 9],
                      t["xf"][:, slot * 128:slot * 128 + 128],
                      start=(kc == 0), stop=(kc == KC - 1)), sr)
    for j in range(NT):
        xt = t["x"][:, j * D:(j + 1) * D]
        stt = t["stat"]
        s1 = stt[:, 8 * j + 0:8 * j + 1]
        s2 = stt[:, 8 * j + 1:8 * j + 2]
        m = stt[:, 8 * j + 2:8 * j + 3]
        msq = stt[:, 8 * j + 3:8 * j + 4]
        var = stt[:, 8 * j + 4:8 * j + 5]
        rstd = stt[:, 8 * j + 5:8 * j + 6]
        mr = stt[:, 8 * j + 6:8 * j + 7]
        sq = t["sq"][:, (j % 2) * D:(j % 2) * D + D]
        dep = b.x_dep[j]
        if dep is not None:
            b.wait("vector", dep[0], dep[1])
            b.wait("scalar", dep[0], dep[1])
        if j < 2:
            for ws, wv in prep_waits:
                b.wait("vector", ws, wv)
                b.wait("scalar", ws, wv)
        b.wait("vector", st_, st_.v - 16)
        b.wait("scalar", st_, st_.v - 16)
        b.sop("vector", lambda e, o=s1, i=xt: e.reduce_sum(o, i, axis=AXX), sv)
        b.sop("scalar", lambda e, i=xt, o=sq, a=s2:
              e.activation(out=o, in_=i, func=AF.Square, accum_out=a), ss)
        b.wait("vector", ss, ss.v)
        b.sop("vector", lambda e, o=m, i=s1: e.tensor_scalar_mul(o, i, 1.0 / D), sv)
        b.sop("vector", lambda e, o=msq, i=m: e.tensor_mul(o, i, i), sv)
        b.sop("vector", lambda e, o=var, i=s2, q=msq:
              e.tensor_scalar(o, i, 1.0 / D, q, ALU.mult, ALU.subtract), sv)
        b.wait("scalar", sv, sv.v)
        b.sop("scalar", lambda e, o=rstd, i=var:
              e.activation(out=o, in_=i, func=AF.Ln, bias=t["epscol"][:, :]), ss)
        b.sop("scalar", lambda e, o=rstd, i=rstd:
              e.activation(out=o, in_=i, func=AF.Exp, scale=-0.5), ss)
        b.wait("vector", ss, ss.v)
        b.sop("vector", lambda e, o=mr, i=m, r=rstd: e.tensor_mul(o, i, r), sv)
        b.sop("vector", lambda e, o=sq, i=xt, r=rstd, z=mr:
              e.tensor_scalar(o, i, r, z, ALU.mult, ALU.subtract), sv)
        b.wait("tensor", sv, sv.v)
        for kc in range(KC):
            n = st_.v
            pcol = 3072 + (n % 2) * 512
            b.wait("tensor", se, se.v - ev_per)
            b.op("tensor", lambda e, kc=kc, p=pcol, i=sq:
                 e.transpose(PS[:, p:p + 128], i[:, kc * 128:(kc + 1) * 128],
                             t["identf"][:, :]), st_)
            if kc % 2 == 0 and "identb" in t:
                b.op("tensor", lambda e:
                     e.matmul(PS[:, 0:128], t["identb"][:, :], t["identb"][:, :],
                              start=True, stop=True), None)
            flush_mm()
            b.wait("vector", st_, st_.v)
            b.op("vector", lambda e, kc=kc, j=j, p=pcol:
                 e.tensor_copy(out=ar[:, XN + kc * TPC + j * 128:XN + kc * TPC + j * 128 + 128],
                               in_=PS[:, p:p + 128]), se)
            if router:
                slot = n % 2
                b.op("vector", lambda e, p=pcol, slot=slot:
                     e.tensor_copy(out=t["xf"][:, slot * 128:slot * 128 + 128],
                                   in_=PS[:, p:p + 128]), se)
                pending = (j, kc, slot, se.v)
    flush_mm()
    return (se, se.v), ((sr, sr.v) if router else None)


def emit_router(b, ln, lnr):
    t = b.t
    PS = t["PS"]
    spe = b.sem("rt_pe")
    sv = b.sem("rt_v")
    ss = b.sem("rt_s")
    b.wait("vector", lnr[0], lnr[1])
    b.op("vector", lambda e: e.tensor_copy(out=t["lgT"], in_=PS[0:9, 2048:3072]), sv)
    b.wait("tensor", sv, sv.v)
    for j in range(NT):
        r = t["route"][:, j * RW:(j + 1) * RW]
        b.op("tensor", lambda e, j=j:
             e.transpose(PS[:, 3072:3081], t["lgT"][:, j * 128:(j + 1) * 128],
                         t["identf"][0:9, 0:9]), spe)
        b.wait("vector", spe, spe.v)
        b.op("vector", lambda e, r=r: e.tensor_copy(out=r[:, 0:9], in_=PS[:, 3072:3081]), sv)
        b.wait("tensor", sv, sv.v)
        lg = r[:, 0:9]
        probs = r[:, 9:17]
        t1 = r[:, 17:25]
        sg = r[:, 25:26]
        nm = r[:, 26:27]
        ssum = r[:, 27:28]
        rs = r[:, 28:29]
        m1 = r[:, 29:30]
        m2 = r[:, 30:31]
        rd = r[:, 31:32]
        b.sop("vector", lambda e, o=nm, i=lg: e.reduce_max(o, i[:, 0:8], axis=AXX), sv)
        b.sop("vector", lambda e, o=nm: e.tensor_scalar_mul(o, o, -1.0), sv)
        b.wait("scalar", sv, sv.v)
        b.sop("scalar", lambda e, o=probs, i=lg, z=nm, a=ssum:
              e.activation(out=o, in_=i[:, 0:8], func=AF.Exp, bias=z, accum_out=a), ss)
        b.sop("scalar", lambda e, o=sg, i=lg:
              e.activation(out=o, in_=i[:, 8:9], func=AF.Sigmoid), ss)
        b.sop("scalar", lambda e, o=sg, i=sg: e.mul(o, i, 0.5), ss)
        b.wait("vector", ss, ss.v)
        b.sop("vector", lambda e, o=rs, i=ssum: e.reciprocal(o, i), sv)
        b.sop("vector", lambda e, o=probs, i=probs, s_=rs: e.tensor_scalar_mul(o, i, s_), sv)
        b.sop("vector", lambda e, o=m1, i=probs: e.reduce_max(o, i, axis=AXX), sv)
        b.sop("vector", lambda e, o=t1, i=probs, s_=m1:
              e.tensor_scalar(o, i, s_, None, ALU.is_equal), sv)
        b.sop("vector", lambda e, o=lg, i=t1, p=probs:
              e.scalar_tensor_tensor(o[:, 0:8], i, -1e30, p, ALU.mult, ALU.add), sv)
        b.sop("vector", lambda e, o=m2, i=lg: e.reduce_max(o, i[:, 0:8], axis=AXX), sv)
        b.sop("vector", lambda e, o=probs, i=lg, s_=m2:
              e.tensor_scalar(o, i[:, 0:8], s_, None, ALU.is_equal), sv)
        b.sop("vector", lambda e, o=rd, a=m1, c=m2: e.tensor_add(o, a, c), sv)
        b.sop("vector", lambda e, o=rd, i=rd: e.reciprocal(o, i), sv)
        b.sop("vector", lambda e, o=t1, i=t1, s_=m1: e.tensor_scalar_mul(o, i, s_), sv)
        b.sop("vector", lambda e, o=t1, i=probs, s_=m2, a=t1:
              e.scalar_tensor_tensor(o, i, s_, a, ALU.mult, ALU.add), sv)
        b.sop("vector", lambda e, o=probs, i=t1, s_=rd:
              e.tensor_scalar(o, i, s_, 0.5, ALU.mult, ALU.mult), sv)
    return sv, sv.v, ss, ss.v


def emit_ffn0(b, ln, rt):
    t = b.t
    ar = t["ar"]
    PS = t["PS"]
    swl = [b.sem("f0_wl0"), b.sem("f0_wl1")]
    swd = b.sem("f0_wd")
    spe = b.sem("f0_pe")
    ssl = b.sem("f0_sl")
    svh = b.sem("f0_h")
    sxe = b.sem("f0_xe")
    rt_v, rt_vv, rt_s, rt_sv = rt
    if not hasattr(b, "down_hist"):
        b.down_hist = []
    if not hasattr(b, "gu_si"):
        b.gu_si = 0

    def gu_slice(par, w):
        c = GU + (w * 2 + par) * D
        return ar[:, c:c + D]

    def emit_gu_load(par, srcs, ci, upe_done):
        if upe_done:
            b.wait("gpsimd", spe, upe_done)
        for w in (0, 1):
            sl = srcs[w](ci)
            dst = gu_slice(par, w).rearrange("p (k q) -> p k q", q=128)
            b.dma("gpsimd", dst, sl, swl[par])

    def emit_gu_compute(h_col):
        par = b.gu_si % 2
        b.gu_si += 1
        b.wait("tensor", swl[par], swl[par].v)
        b.wait("tensor", ssl, ssl.v)
        for kc in range(KC):
            for nh2 in range(2):
                b.op("tensor", lambda e, kc=kc, nh2=nh2, par=par:
                     e.matmul(PS[:, 2048 + nh2 * 512:2560 + nh2 * 512],
                              gu_slice(par, 0)[:, kc * 128:(kc + 1) * 128],
                              ar[:, XN + kc * TPC + nh2 * 512:XN + kc * TPC + nh2 * 512 + 512],
                              start=(kc == 0), stop=(kc == KC - 1)),
                     spe if (kc == KC - 1 and nh2 == 1) else None)
        gdone = spe.v
        b.wait("tensor", svh, svh.v)
        for kc in range(KC):
            for nh2 in range(2):
                b.op("tensor", lambda e, kc=kc, nh2=nh2, par=par:
                     e.matmul(PS[:, 3072 + nh2 * 512:3584 + nh2 * 512],
                              gu_slice(par, 1)[:, kc * 128:(kc + 1) * 128],
                              ar[:, XN + kc * TPC + nh2 * 512:XN + kc * TPC + nh2 * 512 + 512],
                              start=(kc == 0), stop=(kc == KC - 1)),
                     spe if (kc == KC - 1 and nh2 == 1) else None)
        udone = spe.v
        b.wait("scalar", spe, gdone)
        b.wait("scalar", svh, svh.v)
        b.op("scalar", lambda e:
             e.activation(out=ar[:, GS:GS + TPC], in_=PS[:, 2048:3072], func=AF.Silu), ssl)
        b.wait("vector", spe, udone)
        b.wait("vector", ssl, ssl.v)
        b.op("vector", lambda e, h=h_col:
             e.tensor_tensor(out=ar[:, h:h + TPC], in0=PS[:, 3072:4096],
                             in1=ar[:, GS:GS + TPC], op=ALU.mult), svh)
        return udone

    def emit_down(nci, hbase, wbase, scal_fn, wd_ready, xwait=None):
        b.wait("tensor", svh, svh.v)
        b.wait("tensor", swd, wd_ready)
        for st2 in range(16):
            s8, dh = st2 // 2, st2 % 2
            bank = dh * 1024
            if len(b.down_hist) >= 2:
                b.wait("tensor", sxe, b.down_hist[-2])
            for ci2 in range(nci):
                for n5 in range(2):
                    b.op("tensor", lambda e, ci2=ci2, s8=s8, dh=dh, n5=n5, bank=bank,
                         hb=hbase, wb=wbase, nci=nci:
                         e.matmul(PS[:, bank + n5 * 512:bank + n5 * 512 + 512],
                                  ar[:, hb + ci2 * TPC + s8 * 128:hb + ci2 * TPC + s8 * 128 + 128],
                                  ar[:, wb + ci2 * D + dh * 1024 + n5 * 512:wb + ci2 * D + dh * 1024 + n5 * 512 + 512],
                                  start=(ci2 == 0), stop=(ci2 == nci - 1)),
                         spe if (ci2 == nci - 1 and n5 == 1) else None)
            b.wait("vector", spe, spe.v)
            if xwait is not None:
                b.wait("vector", xwait[0], xwait[1])
                xwait = None
            xh = t["x"][:, s8 * D + dh * 1024:s8 * D + dh * 1024 + 1024]
            sc = scal_fn(s8)
            if sc is None:
                b.op("vector", lambda e, p0=bank, xh=xh:
                     e.scalar_tensor_tensor(xh, PS[:, p0:p0 + 1024], 0.5, xh,
                                            ALU.mult, ALU.add), sxe)
            else:
                b.op("vector", lambda e, p0=bank, xh=xh, sc=sc:
                     e.scalar_tensor_tensor(xh, PS[:, p0:p0 + 1024], sc, xh,
                                            ALU.mult, ALU.add), sxe)
            b.down_hist.append(sxe.v)
            b.x_dep[s8] = (sxe, sxe.v)

    b.wait("tensor", rt_v, rt_vv)
    b.wait("tensor", ln[0], ln[1])

    esrcs = {}
    for ee in range(NE):
        esrcs[ee] = (
            lambda ci, ee=ee: t["in_wg"][ee, :, ci * 128:(ci + 1) * 128].rearrange("(k p) q -> p k q", p=128),
            lambda ci, ee=ee: t["in_wu"][ee, :, ci * 128:(ci + 1) * 128].rearrange("(k p) q -> p k q", p=128),
        )

    def wd_load(ee):
        for c0, c1 in ((0, 4), (4, 8), (8, 11)):
            sl = t["in_wd"][ee, c0 * 128:c1 * 128, :].rearrange("(k p) dd -> p k dd", p=128)
            dst = ar[:, WB + c0 * D:WB + c1 * D].rearrange("p (k dd) -> p k dd", dd=D)
            b.dma("gpsimd", dst, sl, swd)

    NE_RUN = 1 if os.environ.get("K_ONE_EXPERT", "") == "1" else NE
    if os.environ.get("K_NO_EXPERTS", "") == "1":
        NE_RUN = 0
    steps = [(ee, ci) for ee in range(NE_RUN) for ci in range(NCI_E)]
    upe_hist = {}
    wd_ready = 0
    b.wait("gpsimd", sxe, sxe.v)
    if NE_RUN:
        emit_gu_load(b.gu_si % 2, esrcs[0], 0, 0)
        emit_gu_load((b.gu_si + 1) % 2, esrcs[0], 1, 0)
        wd_load(0)
        wd_ready = swd.v
    si0 = b.gu_si
    for ee in range(NE_RUN):
        for ci in range(NCI_E):
            si = b.gu_si - si0
            udone = emit_gu_compute(HB + ci * TPC)
            upe_hist[si] = udone
            nxt = si + 2
            if nxt < len(steps):
                emit_gu_load((si0 + nxt) % 2, esrcs[steps[nxt][0]], steps[nxt][1],
                             upe_hist[si])
        emit_down(NCI_E, HB, WB,
                  lambda s8, ee=ee: t["route"][:, s8 * RW + 9 + ee:s8 * RW + 10 + ee],
                  wd_ready,
                  xwait=(rt_v, rt_vv) if ee == 0 else None)
        if ee + 1 < NE_RUN:
            b.wait("gpsimd", spe, spe.v)
            wd_load(ee + 1)
            wd_ready = swd.v

    if os.environ.get("K_NO_SHARED", "") == "1":
        return
    blocks = [list(range(s0, min(s0 + 6, NCI_S))) for s0 in range(0, NCI_S, 6)]
    ssrc = (
        lambda ci: t["in_wsg"][:, ci * 128:(ci + 1) * 128].rearrange("(k p) q -> p k q", p=128),
        lambda ci: t["in_wsu"][:, ci * 128:(ci + 1) * 128].rearrange("(k p) q -> p k q", p=128),
    )

    def wsd_load(bi):
        cis = blocks[bi]
        base = WB + (bi % 2) * 6 * D
        for k, ci in enumerate(cis):
            b.dma("gpsimd", ar[:, base + k * D:base + (k + 1) * D],
                  t["in_wsd"][ci * 128:(ci + 1) * 128, :], swd)

    si0 = b.gu_si
    emit_gu_load(b.gu_si % 2, ssrc, 0, upe_hist.get(len(steps) - 2, 0))
    emit_gu_load((b.gu_si + 1) % 2, ssrc, 1, upe_hist.get(len(steps) - 1, 0))
    b.wait("gpsimd", spe, spe.v)
    wsd_load(0)
    wsd_tgt = swd.v
    down_pe_done = {}
    upe_hist = {}
    ci_global = 0
    for bi, cis in enumerate(blocks):
        hb = HB + (bi % 2) * 6 * TPC
        for k, ci in enumerate(cis):
            si = b.gu_si - si0
            udone = emit_gu_compute(hb + k * TPC)
            upe_hist[si] = udone
            nxt_ci = ci_global + 2
            if nxt_ci < NCI_S:
                emit_gu_load((si0 + si + 2) % 2, ssrc, nxt_ci, upe_hist[si])
            ci_global += 1
        emit_down(len(cis), hb, WB + (bi % 2) * 6 * D,
                  lambda s8: t["route"][:, s8 * RW + 25:s8 * RW + 26],
                  wsd_tgt,
                  xwait=(rt_s, rt_sv) if bi == 0 else None)
        down_pe_done[bi] = spe.v
        if bi + 1 < len(blocks):
            if bi >= 1:
                b.wait("gpsimd", spe, down_pe_done[bi - 1])
            wsd_load(bi + 1)
            wsd_tgt = swd.v



def emit_attn(b, ln, layer):
    t = b.t
    ar = t["ar"]
    PS = t["PS"]
    x = t["x"]
    ISQ = 1.0 / math.sqrt(HD)
    spa = b.sem("at_pe")
    sse = b.sem("at_se")
    sva = b.sem("at_ve")
    sxe = b.sem("f0_xe")
    swq = [b.sem("at_wq0"), b.sem("at_wq1")]
    swf = b.sem("at_wf")
    skh = [b.sem("at_kh0"), b.sem("at_kh1")]
    svh2 = [b.sem("at_vh0"), b.sem("at_vh1")]
    scc = b.sem("at_cc")
    sct = b.sem("at_ct")

    if layer == 0:
        wq_src = lambda oc: t["in_wqkv"][:, oc * 128:(oc + 1) * 128]
        wk_src = lambda oc: t["in_wqkv"][:, D + oc * 128:D + oc * 128 + 128]
        wv_half = lambda kc, vh_: t["in_wqkv"][kc * 128:(kc + 1) * 128,
                                               2 * D + vh_ * TPC:2 * D + (vh_ + 1) * TPC]
        wo_half = lambda kc, dh: t["in_wout"][kc * 128:(kc + 1) * 128, dh * TPC:(dh + 1) * TPC]
        ccki, ccko = t["ccK0i"], t["ccK0o"]
        ccvi, ccvo = t["ccV0i"], t["ccV0o"]
    else:
        wq_src = lambda oc: t["in_wq1"][:, oc * 128:(oc + 1) * 128]
        wk_src = lambda oc: t["in_wk1"][:, oc * 128:(oc + 1) * 128]
        wv_half = lambda kc, vh_: t["in_wv1"][kc * 128:(kc + 1) * 128, vh_ * TPC:(vh_ + 1) * TPC]
        wo_half = lambda kc, dh: t["in_wo1"][kc * 128:(kc + 1) * 128, dh * TPC:(dh + 1) * TPC]
        ccki, ccko = t["ccK1i"], t["ccK1o"]
        ccvi, ccvo = t["ccV1i"], t["ccV1o"]

    brow = ar[0:1, A_PT:A_PT + D]
    ropec = t["sq"][:, 2048:3072]
    ropes = t["sq"][:, 3072:4096]

    def stream(sl):
        return ar[:, A_ST + sl * D:A_ST + (sl + 1) * D]

    def fm_evac(oc, bias_col, dest):
        pc = 1024 * (oc % 2)
        if layer == 0:
            b.op("scalar", lambda e, pc=pc, oc=oc, d=dest:
                 e.activation(out=ar[:, d + oc * TPC:d + (oc + 1) * TPC],
                              in_=PS[:, pc:pc + TPC], func=AF.Identity,
                              bias=bias_col[:, oc:oc + 1]), sse)
            return (sse, sse.v)
        sq = t["sq"]
        b.sop("vector", lambda e, pc=pc:
              e.tensor_tensor(out=sq[:, 0:TPC], in0=PS[:, pc:pc + TPC],
                              in1=ropec, op=ALU.mult), sva)
        b.sop("vector", lambda e, pc=pc:
              e.tensor_tensor(out=sq[0:64, TPC:2 * TPC], in0=PS[64:128, pc:pc + TPC],
                              in1=ropes[0:64, :], op=ALU.mult), sva)
        b.sop("vector", lambda e, pc=pc:
              e.tensor_tensor(out=sq[64:128, TPC:2 * TPC], in0=PS[0:64, pc:pc + TPC],
                              in1=ropes[64:128, :], op=ALU.mult), sva)
        psum_free = sva.v
        b.sop("vector", lambda e, oc=oc, d=dest:
              e.tensor_tensor(out=ar[:, d + oc * TPC:d + (oc + 1) * TPC],
                              in0=sq[:, 0:TPC], in1=sq[:, TPC:2 * TPC],
                              op=ALU.add), sva)
        return (sva, psum_free)

    b.wait("tensor", ln[0], ln[1])
    if layer == 1:
        b.wait("sync", ln[0], ln[1])
        b.dma("sync", ropec, t["in_ropec"], b.sem("at_rp"))
        b.dma("sync", ropes, t["in_ropes"], b.sem("at_rp"))
        srp = b.sem("at_rp")
        b.wait("vector", srp, srp.v)

    spe0 = b.sem("f0_pe")
    b.wait("gpsimd", spe0, spe0.v)
    b.dma("gpsimd", stream(0).rearrange("p (k q) -> p k q", q=128),
          wk_src(0).rearrange("(k p) q -> p k q", p=128), swq[0])
    b.dma("gpsimd", stream(1).rearrange("p (k q) -> p k q", q=128),
          wk_src(1).rearrange("(k p) q -> p k q", p=128), swq[1])
    evac_done = {}
    for oc in range(KC):
        sl = oc % 2
        b.wait("tensor", swq[sl], swq[sl].v)
        if oc >= 2:
            sm, v = evac_done[oc - 2]
            b.wait("tensor", sm, v)
        pc = 1024 * (oc % 2)
        for kc in range(KC):
            for nh2 in range(2):
                b.op("tensor", lambda e, kc=kc, nh2=nh2, sl=sl, pc=pc:
                     e.matmul(PS[:, pc + nh2 * 512:pc + nh2 * 512 + 512],
                              stream(sl)[:, kc * 128:(kc + 1) * 128],
                              ar[:, XN + kc * TPC + nh2 * 512:XN + kc * TPC + nh2 * 512 + 512],
                              start=(kc == 0), stop=(kc == KC - 1)),
                     spa if (kc == KC - 1 and nh2 == 1) else None)
        mm_done = spa.v
        if oc + 2 < KC:
            b.wait("gpsimd", spa, mm_done)
            b.dma("gpsimd", stream(sl).rearrange("p (k q) -> p k q", q=128),
                  wk_src(oc + 2).rearrange("(k p) q -> p k q", p=128), swq[sl])
        b.wait("scalar" if layer == 0 else "vector", spa, mm_done)
        evac_done[oc] = fm_evac(oc, t.get("bk"), A_KV)
    b.wait("sync", sse, sse.v)
    b.wait("sync", sva, sva.v)
    b.dma("sync", ccki.rearrange("(k p) q -> p k q", p=128),
          ar[:, A_KV:A_KV + KC * TPC].rearrange("p (k q) -> p k q", q=TPC), sct)
    k_ct = sct.v
    b.wait("gpsimd", sct, k_ct)
    b.op("gpsimd", lambda e, i=ccki, o=ccko:
         e.collective_compute("AllGather", ALU.bypass,
                              replica_groups=[[0, 1], [2, 3], [4, 5], [6, 7]],
                              ins=[i[:, :]], outs=[o[:, :]]), scc)

    PH = int(os.environ.get("K_ATTN_PH", "9"))
    if PH <= 1:
        b.wait("sync", scc, scc.v)
        return
    if layer == 0:
        b.wait("gpsimd", sct, k_ct)
        b.dma("gpsimd", brow, t["in_bqkv"][2 * D:3 * D].rearrange("(o d) -> o d", o=1), swf)
    vev = {}
    for vh_ in range(2):
        if vh_ == 1:
            b.wait("gpsimd", spa, spa.v)
        for kc in range(KC):
            b.dma("gpsimd", ar[:, A_Q + kc * TPC:A_Q + (kc + 1) * TPC],
                  wv_half(kc, vh_), swf)
        wv_ready = swf.v
        b.wait("tensor", swf, wv_ready)
        if vh_ == 0:
            b.wait("tensor", sct, k_ct)
        for s8 in range(NT):
            pc = 1024 * (s8 % 2)
            if vh_ > 0 or s8 >= 2:
                b.wait("tensor", sva, vev[(vh_ * NT + s8) - 2])
            for kc in range(KC):
                for nh2 in range(2):
                    b.op("tensor", lambda e, kc=kc, nh2=nh2, s8=s8, pc=pc:
                         e.matmul(PS[:, pc + nh2 * 512:pc + nh2 * 512 + 512],
                                  ar[:, XN + kc * TPC + s8 * 128:XN + kc * TPC + s8 * 128 + 128],
                                  ar[:, A_Q + kc * TPC + nh2 * 512:A_Q + kc * TPC + nh2 * 512 + 512],
                                  start=(kc == 0), stop=(kc == KC - 1 and layer == 1)),
                         spa if (layer == 1 and kc == KC - 1 and nh2 == 1) else None)
            if layer == 0:
                for nh2 in range(2):
                    b.op("tensor", lambda e, nh2=nh2, pc=pc, vh_=vh_:
                         e.matmul(PS[:, pc + nh2 * 512:pc + nh2 * 512 + 512],
                                  t["onesrow"][0:1, :],
                                  brow[0:1, vh_ * TPC + nh2 * 512:vh_ * TPC + nh2 * 512 + 512],
                                  start=False, stop=True),
                         spa if nh2 == 1 else None)
            b.wait("vector", spa, spa.v)
            b.op("vector", lambda e, s8=s8, pc=pc, vh_=vh_:
                 e.tensor_copy(out=ar[:, A_KV + s8 * D + vh_ * TPC:A_KV + s8 * D + vh_ * TPC + TPC],
                               in_=PS[:, pc:pc + TPC]), sva)
            vev[vh_ * NT + s8] = sva.v
    b.wait("sync", sva, sva.v)
    b.dma("sync", ccvi.rearrange("(j p) q -> p j q", p=128),
          ar[:, A_KV:A_KV + NT * D].rearrange("p (j q) -> p j q", q=D), sct)
    v_ct = sct.v
    b.wait("gpsimd", sct, v_ct)
    b.op("gpsimd", lambda e, i=ccvi, o=ccvo:
         e.collective_compute("AllGather", ALU.bypass,
                              replica_groups=[[0, 1], [2, 3], [4, 5], [6, 7]],
                              ins=[i[:, :]], outs=[o[:, :]]), scc)
    cc_all = scc.v

    if PH <= 2:
        b.wait("sync", scc, scc.v)
        return
    b.wait("gpsimd", spa, spa.v)
    b.dma("gpsimd", stream(0).rearrange("p (k q) -> p k q", q=128),
          wq_src(0).rearrange("(k p) q -> p k q", p=128), swq[0])
    b.dma("gpsimd", stream(1).rearrange("p (k q) -> p k q", q=128),
          wq_src(1).rearrange("(k p) q -> p k q", p=128), swq[1])
    for oc in range(KC):
        sl = oc % 2
        b.wait("tensor", swq[sl], swq[sl].v)
        if oc >= 2:
            sm, v = evac_done[oc - 2]
            b.wait("tensor", sm, v)
        else:
            b.wait("tensor", sva, sva.v)
        pc = 1024 * (oc % 2)
        for kc in range(KC):
            for nh2 in range(2):
                b.op("tensor", lambda e, kc=kc, nh2=nh2, sl=sl, pc=pc:
                     e.matmul(PS[:, pc + nh2 * 512:pc + nh2 * 512 + 512],
                              stream(sl)[:, kc * 128:(kc + 1) * 128],
                              ar[:, XN + kc * TPC + nh2 * 512:XN + kc * TPC + nh2 * 512 + 512],
                              start=(kc == 0), stop=(kc == KC - 1)),
                     spa if (kc == KC - 1 and nh2 == 1) else None)
        mm_done = spa.v
        if oc + 2 < KC:
            b.wait("gpsimd", spa, mm_done)
            b.dma("gpsimd", stream(sl).rearrange("p (k q) -> p k q", q=128),
                  wq_src(oc + 2).rearrange("(k p) q -> p k q", p=128), swq[sl])
        b.wait("scalar" if layer == 0 else "vector", spa, mm_done)
        evac_done[oc] = fm_evac(oc, t.get("bq"), A_Q)

    if PH <= 3:
        b.wait("sync", scc, scc.v)
        return
    def kh(slot):
        return ar[:, A_ST + slot * D:A_ST + (slot + 1) * D]

    def vh(slot):
        return ar[:, A_ST + (2 + slot) * D:A_ST + (3 + slot) * D]

    def load_head(h):
        slot = h % 2
        for par in range(2):
            src = ccko[par * D + h * 128:par * D + (h + 1) * 128, :].rearrange(
                "p (g q) -> p g q", q=128)
            dst = kh(slot).rearrange("p (g r) -> p g r", r=256)[:, :, par * 128:par * 128 + 128]
            b.dma("sync", dst, src, skh[slot])
            srcv = ccvo[par * TPC:(par + 1) * TPC, h * 128:(h + 1) * 128].rearrange(
                "(g p) q -> p g q", p=128)
            dstv = vh(slot).rearrange("p (g r) -> p g r", r=256)[:, :, par * 128:par * 128 + 128]
            b.dma("sync", dstv, srcv, svh2[slot])

    if os.environ.get("K_DBG_Q", "") == "1":
        sdq = b.sem("at_dbgq")
        b.wait("sync", sse, sse.v)
        b.wait("sync", sva, sva.v)
        b.dma("sync", t["dbgq"], ar[:, A_Q:A_Q + KC * TPC], sdq)
    b.wait("sync", scc, cc_all)
    sm, v = evac_done[KC - 2]
    b.wait("sync", sm, v)
    sm, v = evac_done[KC - 1]
    b.wait("sync", sm, v)
    b.wait("sync", spa, spa.v)
    load_head(0)
    load_head(1)
    b.wait("gpsimd", spa, spa.v)
    for kc in range(KC):
        b.dma("gpsimd", ar[:, XN + kc * TPC:XN + (kc + 1) * TPC], wo_half(kc, 0), swf)
    wo0_ready = swf.v

    exp_done = {}
    av_done = {}
    av_hist = []
    exp_n = 0
    head_pe = {}
    for h in range(NH):
        slot = h % 2
        b.wait("tensor", skh[slot], skh[slot].v)
        b.wait("tensor", svh2[slot], svh2[slot].v)
        head_tail = sva.v
        pend_av = None

        def flush_av():
            nonlocal pend_av, exp_n
            if pend_av is None:
                return
            g, slices = pend_av
            pend_av = None
            b.wait("tensor", sse, exp_done[g])
            if g == 0:
                b.wait("tensor", sva, head_tail)
            for si_, (a0, a1) in enumerate(slices):
                if a1 <= 512:
                    st2, sp2 = (g == 0), (g == 7)
                else:
                    st2, sp2 = (g == 0), (g == KC - 1)
                last = (si_ == len(slices) - 1)
                b.op("tensor", lambda e, g=g, a0=a0, a1=a1, st2=st2, sp2=sp2, slot=slot:
                     e.matmul(PS[:, a0:a1],
                              vh(slot)[:, g * 128:(g + 1) * 128],
                              ar[:, A_PT + (g % 2) * TPC + a0:A_PT + (g % 2) * TPC + a1],
                              start=st2, stop=sp2), None)
                b.op("tensor", lambda e, g=g, a0=a0, a1=a1, st2=st2, sp2=sp2:
                     e.matmul(PS[0:1, 1024 + a0:1024 + a1],
                              t["ones128"][:, :],
                              ar[:, A_PT + (g % 2) * TPC + a0:A_PT + (g % 2) * TPC + a1],
                              start=st2, stop=sp2),
                     spa if last else None)
            av_done[g] = spa.v
            av_hist.append(spa.v)

        for g in range(KC):
            qoff = (g // 2) * 128
            region = 2048 + (g % 2) * 1024
            W = TPC - qoff
            if g >= 2:
                b.wait("tensor", sse, exp_done[g - 2])
            slices = []
            if qoff < 512:
                slices.append((qoff, 512))
            slices.append((max(qoff, 512), 1024))
            for (a0, a1) in slices:
                b.op("tensor", lambda e, g=g, a0=a0, a1=a1, region=region, h=h,
                     qoff=qoff, slot=slot:
                     e.matmul(PS[:, region + a0:region + a1],
                              kh(slot)[:, g * 128:(g + 1) * 128],
                              ar[:, A_Q + h * TPC + a0:A_Q + h * TPC + a1],
                              start=True, stop=(a0 != qoff)), None)
            b.op("tensor", lambda e, g=g, region=region, qoff=qoff:
                 e.matmul(PS[:, region + qoff:region + qoff + 128],
                          t["identb"][:, :],
                          t["masks"][:, (g % 2) * 128:(g % 2) * 128 + 128],
                          start=False, stop=True), spa)
            smm = spa.v
            b.wait("scalar", spa, smm)
            if exp_n >= 2:
                b.wait("scalar", spa, av_hist[exp_n - 2])
            b.op("scalar", lambda e, g=g, region=region, qoff=qoff, W=W:
                 e.activation(out=ar[:, A_PT + (g % 2) * TPC + qoff:A_PT + (g % 2) * TPC + qoff + W],
                              in_=PS[:, region + qoff:region + qoff + W],
                              func=AF.Exp, scale=ISQ), sse)
            exp_done[g] = sse.v
            t["_at_last_exp"] = sse.v
            exp_n += 1
            flush_av()
            pend_av = (g, slices)
        flush_av()
        if h + 1 < NH:
            b.wait("sync", spa, head_pe.get(h - 1, 0))
            load_head(h + 1)
        b.wait("vector", spa, av_done[KC - 1])
        lr = t["sq"][0:1, 0:TPC]
        lb = ar[0:1, A_LB:A_LB + TPC]
        b.sop("vector", lambda e, lr=lr: e.tensor_copy(out=lr, in_=PS[0:1, 1024:2048]), sva)
        b.sop("vector", lambda e, lr=lr: e.reciprocal(lr, lr), sva)
        b.sop("vector", lambda e, lr=lr, lb=lb: e.tensor_copy(out=lb, in_=lr), sva)
        lb_done = sva.v
        b.wait("tensor", sva, lb_done)
        for n2 in range(2):
            b.op("tensor", lambda e, n2=n2, lb=lb:
                 e.matmul(PS[:, 1024 + n2 * 512:1024 + n2 * 512 + 512],
                          t["onesrow"][0:1, :], lb[0:1, n2 * 512:(n2 + 1) * 512],
                          start=True, stop=True), spa)
        bc_done = spa.v
        head_pe[h] = spa.v
        b.wait("vector", spa, bc_done)
        b.sop("vector", lambda e:
              e.tensor_copy(out=ar[:, A_LI:A_LI + TPC], in_=PS[:, 1024:2048]), sva)
        b.sop("vector", lambda e, h=h:
              e.tensor_tensor(out=ar[:, A_KV + h * TPC:A_KV + (h + 1) * TPC],
                              in0=PS[:, 0:TPC], in1=ar[:, A_LI:A_LI + TPC],
                              op=ALU.mult), sva)

    if PH <= 4:
        return
    if layer == 0:
        b.wait("gpsimd", sse, sse.v)
        b.dma("gpsimd", brow, t["in_bout"].rearrange("(o d) -> o d", o=1), swf)
    oev = {}
    for dh in range(2 if PH >= 6 else 1):
        if dh == 1 and PH >= 7:
            b.wait("gpsimd", sse, t["_at_last_exp"])
            for kc in range(KC):
                b.dma("gpsimd", ar[:, A_Q + kc * TPC:A_Q + (kc + 1) * TPC],
                      wo_half(kc, 1), swf)
        wo_ready = swf.v
        b.wait("tensor", swf, wo_ready)
        if dh == 0:
            b.wait("tensor", sva, sva.v)
        wbase = XN if (dh == 0 or PH < 7) else A_Q
        for s8 in range(NT):
            pc = 1024 * (s8 % 2)
            st3 = dh * NT + s8
            if st3 >= 2:
                b.wait("tensor", sxe, oev[st3 - 2])
            for kc in range(KC):
                for nh2 in range(2):
                    b.op("tensor", lambda e, kc=kc, nh2=nh2, s8=s8, pc=pc, wbase=wbase:
                         e.matmul(PS[:, pc + nh2 * 512:pc + nh2 * 512 + 512],
                                  ar[:, A_KV + kc * TPC + s8 * 128:A_KV + kc * TPC + s8 * 128 + 128],
                                  ar[:, wbase + kc * TPC + nh2 * 512:wbase + kc * TPC + nh2 * 512 + 512],
                                  start=(kc == 0), stop=(kc == KC - 1 and layer == 1)),
                         spa if (layer == 1 and kc == KC - 1 and nh2 == 1) else None)
            if layer == 0:
                for nh2 in range(2):
                    b.op("tensor", lambda e, nh2=nh2, pc=pc, dh=dh:
                         e.matmul(PS[:, pc + nh2 * 512:pc + nh2 * 512 + 512],
                                  t["onesrow"][0:1, :],
                                  brow[0:1, dh * TPC + nh2 * 512:dh * TPC + nh2 * 512 + 512],
                                  start=False, stop=True),
                         spa if nh2 == 1 else None)
            b.wait("vector", spa, spa.v)
            dep = b.x_dep[s8]
            if dep is not None and dh == 0:
                b.wait("vector", dep[0], dep[1])
            b.op("vector", lambda e, s8=s8, pc=pc, dh=dh:
                 e.tensor_tensor(out=x[:, s8 * D + dh * TPC:s8 * D + dh * TPC + TPC],
                                 in0=PS[:, pc:pc + TPC],
                                 in1=x[:, s8 * D + dh * TPC:s8 * D + dh * TPC + TPC],
                                 op=ALU.add), sxe)
            oev[st3] = sxe.v
            b.x_dep[s8] = (sxe, sxe.v)



def emit_mat1(b):
    t = b.t
    ar = t["ar"]
    PS = t["PS"]
    sld = b.sem("m1_ld")
    sst = b.sem("m1_st")
    str_ = b.sem("m1_tr")
    sve = b.sem("m1_ve")
    scc = b.sem("m1_cc")
    sxe = b.sem("f0_xe")
    prep = b.sem("prep")
    b.wait("vector", prep, prep.v)
    pve = b.sem("prep_ve")
    b.wait("vector", pve, pve.v)
    b.wait("gpsimd", sxe, sxe.v)
    b.wait("sync", sxe, sxe.v)

    CB = [0, 11008]
    TT = [22016, 33024]
    WW = 44032
    MM = [55040 + 0, 55040 + 5504]
    MK = 55040

    shards = [
        ("g", t["in_gidx"], t["gbf"], 5504, 256, t["cgi"], 2),
        ("u", t["in_uidx"], t["ubf"], 5504, 256, t["cui"], 2),
        ("d", t["in_didx"], t["dbf"], 2048, 688, t["cdi"], 6),
    ]
    for i, (nm, src, dst, R, C, _, _n) in enumerate(shards):
        sl = i % 2
        if i >= 2:
            b.wait("gpsimd", sst, 16 * i)
        ncols = (R // 128) * C
        b.dma("gpsimd", ar[:, CB[sl]:CB[sl] + ncols].rearrange("p (tt c) -> p tt c", c=C),
              src.rearrange("(tt p) c -> p tt c", p=128), sld)
        b.wait("sync", sld, sld.v)
        b.dma("sync", dst.rearrange("(tt p) c -> p tt c", p=128),
              ar[:, CB[sl]:CB[sl] + ncols].rearrange("p (tt c) -> p tt c", c=C), sst)

    lutb = t["lutb"]
    tslot = 0
    tr_hist = []
    sel_hist = []

    def do_tile(bfsrc, r0, r1, c0, c1, width, out_dram_rows, cdst, wslot, trow0=0):
        nonlocal tslot
        sl = tslot % 2
        tslot += 1
        if len(tr_hist) >= 2:
            b.wait("sync", sve, sel_hist[-2])
        b.dmat("sync", ar[:, TT[sl]:TT[sl] + width], bfsrc[r0:r1, c0:c1], str_)
        tr_hist.append(str_.v)
        b.wait("vector", str_, str_.v)
        for k in range(16):
            mb = MK if width > 2048 else MM[k % 2] if False else MK
            b.wait("tensor", sve, sve.v)
            b.op("tensor", lambda e:
                 e.matmul(PS[:, 0:128], t["identb"][:, :], t["identb"][:, :],
                          start=True, stop=True), None)
            b.sop("vector", lambda e, k=k, sl=sl, width=width, mb=mb:
                  e.tensor_scalar(ar[:, mb:mb + width], ar[:, TT[sl]:TT[sl] + width],
                                  float(k), None, ALU.is_equal), sve)
            if k == 0:
                b.sop("vector", lambda e, k=k, width=width, mb=mb:
                      e.tensor_scalar(ar[:, WW:WW + width], ar[:, mb:mb + width],
                                      lutb[:, k:k + 1], None, ALU.mult), sve)
            else:
                b.sop("vector", lambda e, k=k, width=width, mb=mb:
                      e.scalar_tensor_tensor(ar[:, WW:WW + width], ar[:, mb:mb + width],
                                             lutb[:, k:k + 1], ar[:, WW:WW + width],
                                             ALU.mult, ALU.add), sve)
        sel_hist.append(sve.v)
        b.wait("sync", sve, sve.v)
        b.dma("sync", cdst[out_dram_rows[0]:out_dram_rows[1], :],
              ar[trow0:128, WW:WW + width], sst)

    b.wait("sync", sst, 16 * 3)
    gathers = {"g": (t["cgi"], t["w1gT"]), "u": (t["cui"], t["w1uT"]),
               "d": (t["cdi"], t["w1dT"])}
    for nm, src, bf, R, C, cdst, ntile in shards:
        if nm in ("g", "u"):
            for dt in range(2):
                do_tile(bf, 0, R, dt * 128, (dt + 1) * 128, R, (dt * 128, (dt + 1) * 128), cdst, None)
        else:
            for q in range(6):
                c0 = q * 128 if q < 5 else C - 128
                rows = (q * 128, min((q + 1) * 128, C))
                do_tile(bf, 0, R, c0, c0 + 128, R, rows, cdst, None,
                        trow0=0 if q < 5 else 128 - (rows[1] - rows[0]))
        ins_, outs_ = gathers[nm]
        b.wait("gpsimd", sst, sst.v)
        b.op("gpsimd", lambda e, i_=ins_, o_=outs_:
             e.collective_compute("AllGather", ALU.bypass,
                                  replica_groups=[[0, 1, 2, 3, 4, 5, 6, 7]],
                                  ins=[i_[:, :]], outs=[o_[:, :]]), scc)


    return scc, scc.v


def emit_ffn1(b, ln, mat):
    t = b.t
    ar = t["ar"]
    PS = t["PS"]
    x = t["x"]
    swl = [b.sem("f1_wl0"), b.sem("f1_wl1")]
    swd = b.sem("f1_wd")
    spe = b.sem("f0_pe")
    ssl = b.sem("f0_sl")
    svh = b.sem("f0_h")
    sxe = b.sem("f0_xe")
    if not hasattr(b, "down_hist"):
        b.down_hist = []
    if not hasattr(b, "gu_si"):
        b.gu_si = 0

    def gu_slice(par, w):
        c = GU + (w * 2 + par) * D
        return ar[:, c:c + D]

    def gu_load(par, ci, upe_done):
        if upe_done:
            b.wait("sync", spe, upe_done)
        for w, src in ((0, t["w1gT"]), (1, t["w1uT"])):
            sl = src[:, ci * 128:(ci + 1) * 128].rearrange("(k p) q -> p k q", p=128)
            b.dma("sync", gu_slice(par, w).rearrange("p (k q) -> p k q", q=128), sl, swl[par])

    def gu_compute(h_col):
        par = b.gu_si % 2
        b.gu_si += 1
        b.wait("tensor", swl[par], swl[par].v)
        b.wait("tensor", ssl, ssl.v)
        for kc in range(KC):
            for nh2 in range(2):
                b.op("tensor", lambda e, kc=kc, nh2=nh2, par=par:
                     e.matmul(PS[:, 2048 + nh2 * 512:2560 + nh2 * 512],
                              gu_slice(par, 0)[:, kc * 128:(kc + 1) * 128],
                              ar[:, XN + kc * TPC + nh2 * 512:XN + kc * TPC + nh2 * 512 + 512],
                              start=(kc == 0), stop=(kc == KC - 1)),
                     spe if (kc == KC - 1 and nh2 == 1) else None)
        gdone = spe.v
        b.wait("tensor", svh, svh.v)
        for kc in range(KC):
            for nh2 in range(2):
                b.op("tensor", lambda e, kc=kc, nh2=nh2, par=par:
                     e.matmul(PS[:, 3072 + nh2 * 512:3584 + nh2 * 512],
                              gu_slice(par, 1)[:, kc * 128:(kc + 1) * 128],
                              ar[:, XN + kc * TPC + nh2 * 512:XN + kc * TPC + nh2 * 512 + 512],
                              start=(kc == 0), stop=(kc == KC - 1)),
                     spe if (kc == KC - 1 and nh2 == 1) else None)
        udone = spe.v
        b.wait("scalar", spe, gdone)
        b.wait("scalar", svh, svh.v)
        b.op("scalar", lambda e:
             e.activation(out=ar[:, GS:GS + TPC], in_=PS[:, 2048:3072], func=AF.Silu), ssl)
        b.wait("vector", spe, udone)
        b.wait("vector", ssl, ssl.v)
        b.op("vector", lambda e, h=h_col:
             e.tensor_tensor(out=ar[:, h:h + TPC], in0=PS[:, 3072:4096],
                             in1=ar[:, GS:GS + TPC], op=ALU.mult), svh)
        return udone

    def down(nci, hbase, wbase):
        b.wait("tensor", svh, svh.v)
        b.wait("tensor", swd, swd.v)
        for st2 in range(16):
            s8, dh = st2 // 2, st2 % 2
            bank = dh * 1024
            if len(b.down_hist) >= 2:
                b.wait("tensor", sxe, b.down_hist[-2])
            for ci2 in range(nci):
                for n5 in range(2):
                    b.op("tensor", lambda e, ci2=ci2, s8=s8, dh=dh, n5=n5, bank=bank,
                         hb=hbase, wb=wbase, nci=nci:
                         e.matmul(PS[:, bank + n5 * 512:bank + n5 * 512 + 512],
                                  ar[:, hb + ci2 * TPC + s8 * 128:hb + ci2 * TPC + s8 * 128 + 128],
                                  ar[:, wb + ci2 * D + dh * 1024 + n5 * 512:wb + ci2 * D + dh * 1024 + n5 * 512 + 512],
                                  start=(ci2 == 0), stop=(ci2 == nci - 1)),
                         spe if (ci2 == nci - 1 and n5 == 1) else None)
            b.wait("vector", spe, spe.v)
            xh = x[:, s8 * D + dh * 1024:s8 * D + dh * 1024 + 1024]
            b.op("vector", lambda e, p0=bank, xh=xh:
                 e.scalar_tensor_tensor(xh, PS[:, p0:p0 + 1024], 0.5, xh,
                                        ALU.mult, ALU.add), sxe)
            b.down_hist.append(sxe.v)
            b.x_dep[s8] = (sxe, sxe.v)

    blocks = [list(range(s0, min(s0 + 6, NCI_1))) for s0 in range(0, NCI_1, 6)]

    def wd_load(bi):
        base = WB + (bi % 2) * 6 * D
        for kk, ci in enumerate(blocks[bi]):
            b.dma("sync", ar[:, base + kk * D:base + (kk + 1) * D],
                  t["w1dT"][ci * 128:(ci + 1) * 128, :], swd)

    b.wait("tensor", ln[0], ln[1])
    b.wait("sync", mat[0], mat[1])
    b.wait("sync", sxe, sxe.v)
    b.wait("sync", svh, svh.v)
    si0 = b.gu_si
    gu_load(b.gu_si % 2, 0, 0)
    gu_load((b.gu_si + 1) % 2, 1, 0)
    wd_load(0)
    wd_tgt = swd.v
    upe_hist = {}
    down_pe = {}
    ci_global = 0
    for bi, cis in enumerate(blocks):
        hb = HB + (bi % 2) * 6 * TPC
        for kk, ci in enumerate(cis):
            si = b.gu_si - si0
            udone = gu_compute(hb + kk * TPC)
            upe_hist[si] = udone
            nxt = ci_global + 2
            if nxt < NCI_1:
                gu_load((si0 + si + 2) % 2, nxt, upe_hist[si])
            ci_global += 1
        down(len(cis), hb, WB + (bi % 2) * 6 * D)
        down_pe[bi] = spe.v
        if bi + 1 < len(blocks):
            if bi >= 1:
                b.wait("sync", spe, down_pe[bi - 1])
            wd_load(bi + 1)



def declare_common(nc, b, parts):
    t = b.t
    ctx = b.ctx
    e = ctx.enter_context
    t["xin"] = nc.dram_tensor("xin", [NT, 128, D], F32, kind="ExternalInput").ap()
    t["xout"] = nc.dram_tensor("xout", [NT, 128, D], F32, kind="ExternalOutput").ap()
    if os.environ.get("K_DBG_Q", "") == "1":
        t["dbgq"] = nc.dram_tensor("dbgq", [128, KC * TPC], BF16, kind="ExternalOutput").ap()
    if os.environ.get("K_DBG_X", "") == "1":
        t["dbgx"] = nc.dram_tensor("dbgx", [6, NT, 128, D], F32, kind="ExternalOutput").ap()
    if os.environ.get("K_DEBUG", "") == "1":
        t["dbg_stat"] = nc.dram_tensor("dbg_stat", [128, 64], F32, kind="ExternalOutput").ap()
        t["dbg_route"] = nc.dram_tensor("dbg_route", [128, NT * RW], F32, kind="ExternalOutput").ap()
        t["dbg_xn"] = nc.dram_tensor("dbg_xn", [128, KC * TPC], BF16, kind="ExternalOutput").ap()
    t["in_identf"] = nc.dram_tensor("identf", [128, 128], F32, kind="ExternalInput").ap()
    t["x"] = e(nc.sbuf_tensor("x", [128, NT * D], F32))
    t["ar"] = e(nc.sbuf_tensor("ar", [128, ARENA], BF16))
    t["PS"] = e(nc.psum_tensor("PS", [128, 4096], F32))
    t["sq"] = e(nc.sbuf_tensor("sq", [128, 2 * D], F32))
    t["stat"] = e(nc.sbuf_tensor("stat", [128, 64], F32))
    t["identf"] = e(nc.sbuf_tensor("identfs", [128, 128], F32))
    t["epscol"] = e(nc.sbuf_tensor("epscol", [128, 1], F32))
    t["prow"] = t["sq"][0:1, 2048:4096]
    if {"attn0", "attn1", "mat1", "ffn1"} & set(parts):
        t["in_identb"] = nc.dram_tensor("identb", [128, 128], BF16, kind="ExternalInput").ap()
        t["in_onesrow"] = nc.dram_tensor("onesrow", [1, 128], BF16, kind="ExternalInput").ap()
        t["in_ones128"] = nc.dram_tensor("ones128", [128, 1], BF16, kind="ExternalInput").ap()
        t["in_onesrowf"] = nc.dram_tensor("onesrowf", [1, 128], F32, kind="ExternalInput").ap()
        t["identb"] = e(nc.sbuf_tensor("identbs", [128, 128], BF16))
        t["onesrow"] = e(nc.sbuf_tensor("onesrows", [1, 128], BF16))
        t["ones128"] = e(nc.sbuf_tensor("ones128s", [128, 1], BF16))
        t["onesrowf"] = e(nc.sbuf_tensor("onesrowfs", [1, 128], F32))
    if {"attn0", "attn1"} & set(parts):
        t["in_masks"] = nc.dram_tensor("masks", [2, 128, 128], BF16, kind="ExternalInput").ap()
        t["masks"] = e(nc.sbuf_tensor("maskss", [128, 256], BF16))
    if "attn1" in parts:
        t["in_ropec"] = nc.dram_tensor("ropec", [128, TPC], F32, kind="ExternalInput").ap()
        t["in_ropes"] = nc.dram_tensor("ropes", [128, TPC], F32, kind="ExternalInput").ap()
    if "ffn0" in parts:
        t["in_gate_w"] = nc.dram_tensor("gate_w", [8, D], F32, kind="ExternalInput").ap()
        t["in_sgw"] = nc.dram_tensor("sgw", [D, 1], F32, kind="ExternalInput").ap()
        t["in_wg"] = nc.dram_tensor("wg", [NE, D, IE], F32, kind="ExternalInput").ap()
        t["in_wu"] = nc.dram_tensor("wu", [NE, D, IE], F32, kind="ExternalInput").ap()
        t["in_wd"] = nc.dram_tensor("wd", [NE, IE, D], F32, kind="ExternalInput").ap()
        t["in_wsg"] = nc.dram_tensor("wsg", [D, ISH], F32, kind="ExternalInput").ap()
        t["in_wsu"] = nc.dram_tensor("wsu", [D, ISH], F32, kind="ExternalInput").ap()
        t["in_wsd"] = nc.dram_tensor("wsd", [ISH, D], F32, kind="ExternalInput").ap()
        t["gwraw"] = t["sq"][0:8, 0:D]
        t["sgwc"] = e(nc.sbuf_tensor("sgwc", [128, KC], F32))
        t["gwt"] = e(nc.sbuf_tensor("gwt", [128, 9 * KC], F32))
        t["xf"] = e(nc.sbuf_tensor("xf", [128, 256], F32))
        t["lgT"] = t["sq"][0:9, 0:TPC]
        t["route"] = e(nc.sbuf_tensor("route", [128, NT * RW], F32))
    for L in (0, 1):
        if f"attn{L}" in parts:
            t[f"ccK{L}i"] = nc.dram_tensor(f"ccK{L}i", [D, TPC], BF16, kind="Internal").ap()
            t[f"ccK{L}o"] = nc.dram_tensor(f"ccK{L}o", [2 * D, TPC], BF16, kind="Internal").ap()
            t[f"ccV{L}i"] = nc.dram_tensor(f"ccV{L}i", [TPC, D], BF16, kind="Internal").ap()
            t[f"ccV{L}o"] = nc.dram_tensor(f"ccV{L}o", [2 * TPC, D], BF16, kind="Internal").ap()
    if "attn0" in parts:
        t["in_bqkv"] = nc.dram_tensor("bqkv", [3 * D], F32, kind="ExternalInput").ap()
        t["in_bout"] = nc.dram_tensor("bout", [D], F32, kind="ExternalInput").ap()
        t["in_wqkv"] = nc.dram_tensor("wqkv", [D, 3 * D], F32, kind="ExternalInput").ap()
        t["in_wout"] = nc.dram_tensor("wout", [D, D], F32, kind="ExternalInput").ap()
        t["bq"] = e(nc.sbuf_tensor("bq", [128, KC], F32))
        t["bk"] = e(nc.sbuf_tensor("bk", [128, KC], F32))
    if "attn1" in parts:
        t["in_wq1"] = nc.dram_tensor("wq1", [D, D], F32, kind="ExternalInput").ap()
        t["in_wk1"] = nc.dram_tensor("wk1", [D, D], F32, kind="ExternalInput").ap()
        t["in_wv1"] = nc.dram_tensor("wv1", [D, D], F32, kind="ExternalInput").ap()
        t["in_wo1"] = nc.dram_tensor("wo1", [D, D], F32, kind="ExternalInput").ap()
    if "mat1" in parts:
        t["in_lut"] = nc.dram_tensor("lut", [16], F32, kind="ExternalInput").ap()
        t["in_gidx"] = nc.dram_tensor("gidx", [I1, 256], I32, kind="ExternalInput").ap()
        t["in_uidx"] = nc.dram_tensor("uidx", [I1, 256], I32, kind="ExternalInput").ap()
        t["in_didx"] = nc.dram_tensor("didx", [D, 688], I32, kind="ExternalInput").ap()
        t["lutrow"] = e(nc.sbuf_tensor("lutrow", [1, 16], F32))
        t["lutb"] = e(nc.sbuf_tensor("lutb", [128, 16], F32))
        t["gbf"] = nc.dram_tensor("gbf", [I1, 256], BF16, kind="Internal").ap()
        t["ubf"] = nc.dram_tensor("ubf", [I1, 256], BF16, kind="Internal").ap()
        t["dbf"] = nc.dram_tensor("dbf", [D, 688], BF16, kind="Internal").ap()
        t["cgi"] = nc.dram_tensor("cgi", [256, I1], BF16, kind="Internal").ap()
        t["cui"] = nc.dram_tensor("cui", [256, I1], BF16, kind="Internal").ap()
        t["cdi"] = nc.dram_tensor("cdi", [688, D], BF16, kind="Internal").ap()
    if {"mat1", "ffn1"} & set(parts):
        t["w1gT"] = nc.dram_tensor("w1gT", [D, I1], BF16, kind="Internal").ap()
        t["w1uT"] = nc.dram_tensor("w1uT", [D, I1], BF16, kind="Internal").ap()
        t["w1dT"] = nc.dram_tensor("w1dT", [I1, D], BF16, kind="Internal").ap()


def emit_load_x(b):
    s = b.sem("x_in")
    for j in range(NT):
        b.dma("sync", b.t["x"][:, j * D:(j + 1) * D], b.t["xin"][j], s)
    for j in range(NT):
        b.x_dep[j] = (s, s.v)


def emit_store_x(b):
    s = b.sem("x_out")
    for j in range(NT):
        dep = b.x_dep[j]
        if dep is not None:
            b.wait("sync", dep[0], dep[1])
        b.dma("sync", b.t["xout"][j], b.t["x"][:, j * D:(j + 1) * D], s)
    if os.environ.get("K_DEBUG", "") == "1":
        b.dma("sync", b.t["dbg_stat"], b.t["stat"][:, :], s)
        b.dma("sync", b.t["dbg_route"], b.t["route"][:, :], s)
        b.dma("sync", b.t["dbg_xn"], b.t["ar"][:, XN:XN + KC * TPC], s)
    b.wait("sync", s, s.v)


def build_program(parts):
    nc = bass.Bass("TRN2", target_bir_lowering=False)
    ctx = ExitStack()
    with ctx:
        b = B(nc, ctx)
        declare_common(nc, b, parts)
        ps_, ppe_, pve_ = emit_prep(b, parts)
        emit_load_x(b)
        prep_w = [(ps_, ps_.v), (ppe_, ppe_.v), (pve_, pve_.v)]
        first = True
        dbgx_i = [0]

        def dump_x():
            if os.environ.get("K_DBG_X", "") != "1":
                return
            sdx = b.sem("dbgx")
            for j in range(NT):
                dep = b.x_dep[j]
                if dep is not None:
                    b.wait("sync", dep[0], dep[1])
                b.dma("sync", b.t["dbgx"][dbgx_i[0], j],
                      b.t["x"][:, j * D:(j + 1) * D], sdx)
            dbgx_i[0] += 1
        for p in parts:
            if p == "ffn0":
                ln, lnr = emit_ln(b, prep_w if first else (), router=True)
                rt = emit_router(b, ln, lnr)
                emit_ffn0(b, ln, rt)
            elif p in ("attn0", "attn1"):
                ln, _ = emit_ln(b, prep_w if first else (), router=False)
                emit_attn(b, ln, 0 if p == "attn0" else 1)
            elif p == "mat1":
                b.t["_mat"] = emit_mat1(b)
            elif p == "ffn1":
                ln, _ = emit_ln(b, prep_w if first else (), router=False)
                emit_ffn1(b, ln, b.t["_mat"])
            else:
                raise NotImplementedError(p)
            if p != "mat1":
                dump_x()
            first = False
        emit_store_x(b)
        block = ctx.enter_context(nc.Block())
        b.replay(block)
    return nc



def make_constants(core, parts):
    d = {}
    par = core % 2
    d["identf"] = np.eye(128, dtype=np.float32)
    if {"attn0", "attn1", "mat1", "ffn1"} & set(parts):
        d["identb"] = np.eye(128, dtype=BF)
        d["onesrow"] = np.ones((1, 128), dtype=BF)
        d["ones128"] = np.ones((128, 1), dtype=BF)
        d["onesrowf"] = np.ones((1, 128), dtype=np.float32)
    if {"attn0", "attn1"} & set(parts):
        tri = np.where(np.arange(128)[None, :] < np.arange(128)[:, None],
                       np.float32(-1e9), np.float32(0.0))
        zero = np.zeros((128, 128), np.float32)
        full = np.full((128, 128), -1e9, np.float32)
        meven = tri if par == 0 else zero
        modd = full if par == 0 else tri
        d["masks"] = np.stack([meven, modd]).astype(BF)
    if "attn1" in parts:
        inv = 1.0 / (10000.0 ** (np.arange(0, HD, 2, dtype=np.float32) / HD))
        tiles = 2 * np.arange(NT) + par
        pos = (tiles[:, None] * 128 + np.arange(128)[None, :]).reshape(-1).astype(np.float32)
        fr = np.outer(inv, pos)
        d["ropec"] = np.concatenate([np.cos(fr), np.cos(fr)], 0).astype(np.float32)
        sn = np.sin(fr)
        d["ropes"] = np.concatenate([-sn, sn], 0).astype(np.float32)
    return d


def shard_x(x, core):
    seq, par = core // 2, core % 2
    tiles = []
    for j in range(NT):
        g = 2 * j + par
        tiles.append(np.asarray(x[seq, g * 128:(g + 1) * 128, :], dtype=np.float32))
    return np.stack(tiles)


def unshard_x(shards):
    out = np.empty((4, 2048, 2048), dtype=np.float32)
    for c in range(NCORE):
        seq, par = c // 2, c % 2
        for j in range(NT):
            g = 2 * j + par
            out[seq, g * 128:(g + 1) * 128, :] = shards[c][j]
    return out


def ffn0_inputs(inputs):
    return {
        "gate_w": np.ascontiguousarray(inputs["L0_ffn_gate_w"], np.float32),
        "sgw": np.ascontiguousarray(inputs["L0_ffn_shared_gate_w"], np.float32),
        "wg": np.ascontiguousarray(inputs["L0_ffn_w_gate"], np.float32),
        "wu": np.ascontiguousarray(inputs["L0_ffn_w_up"], np.float32),
        "wd": np.ascontiguousarray(inputs["L0_ffn_w_down"], np.float32),
        "wsg": np.ascontiguousarray(inputs["L0_ffn_sw_gate"], np.float32),
        "wsu": np.ascontiguousarray(inputs["L0_ffn_sw_up"], np.float32),
        "wsd": np.ascontiguousarray(inputs["L0_ffn_sw_down"], np.float32),
    }


def run_launch(parts, x, inputs):
    nc = build_program(parts)
    in_maps = []
    for c in range(NCORE):
        m = {"xin": shard_x(x, c)}
        m.update(make_constants(c, parts))
        if "ffn0" in parts:
            m.update(ffn0_inputs(inputs))
        if "attn0" in parts:
            m["bqkv"] = np.ascontiguousarray(inputs["L0_attn_b_qkv"], np.float32)
            m["bout"] = np.ascontiguousarray(inputs["L0_attn_b_out"], np.float32)
            m["wqkv"] = np.ascontiguousarray(inputs["L0_attn_w_qkv"], np.float32)
            m["wout"] = np.ascontiguousarray(inputs["L0_attn_w_out"], np.float32)
        if "mat1" in parts:
            m["lut"] = np.ascontiguousarray(inputs["L1_ffn_lut"], np.float32)
            m["gidx"] = np.ascontiguousarray(inputs["L1_ffn_gate_idx"][:, c * 256:(c + 1) * 256], np.int32)
            m["uidx"] = np.ascontiguousarray(inputs["L1_ffn_up_idx"][:, c * 256:(c + 1) * 256], np.int32)
            m["didx"] = np.ascontiguousarray(inputs["L1_ffn_down_idx"][:, c * 688:(c + 1) * 688], np.int32)
        if "attn1" in parts:
            m["wq1"] = np.ascontiguousarray(inputs["L1_attn_Wq"], np.float32)
            m["wk1"] = np.ascontiguousarray(inputs["L1_attn_Wk"], np.float32)
            m["wv1"] = np.ascontiguousarray(inputs["L1_attn_Wv"], np.float32)
            m["wo1"] = np.ascontiguousarray(inputs["L1_attn_Wo"], np.float32)
        in_maps.append(m)
    res = run_bass_kernel_spmd(nc, in_maps, core_ids=list(range(NCORE)))
    return unshard_x([res.results[c]["xout"] for c in range(NCORE)])


LAUNCH_PLAN = [["ffn0", "attn0", "ffn0", "mat1", "ffn1", "attn1", "ffn1"]]


def kernel(**inputs):
    x = np.asarray(inputs["L0_attn_x"], np.float32)
    for parts in LAUNCH_PLAN:
        x = run_launch(parts, x, inputs)
    return x
